# version 03
# baseline (speedup 1.0000x reference)
import numpy as np
import ml_dtypes
import concourse.bass as bass
import concourse.mybir as mybir
import bass_rust
from concourse.bass_utils import run_bass_kernel_spmd

F32 = mybir.dt.float32
BF16 = mybir.dt.bfloat16
FP8 = mybir.dt.float8e4
U32 = mybir.dt.uint32
AF = mybir.ActivationFunctionType
ALU = mybir.AluOpType
IOff = bass_rust.IndirectOffsetOnAxis

DM, DMC = 1024, 8
H, HC = 4096, 32
DI, DIC = 2048, 16
S = 16
TQ, T = 1024, 4096
NW = 2
XSW = 1028

ENGINES = ("sync", "scalar", "vector", "tensor", "gpsimd")


class Sem:
    def __init__(self, nc, name):
        self.h = nc.alloc_semaphore(name)
        self.n = 0
        self._name = name

    def b(self, k):
        self.n += k
        return self.h


class _ProbeInst:
    def then_inc(self, *a, **k):
        return self


class _Probe:
    def __getattr__(self, name):
        return lambda *a, **k: _ProbeInst()


class Tracker:

    def __init__(self, nc):
        self.nc = nc
        self.csem = {e: Sem(nc, "C_" + e) for e in ENGINES}
        self.dsem = {e: Sem(nc, "D_" + e) for e in ENGINES}
        self.cc = Sem(nc, "CCs")
        self.clock = {e: {} for e in ENGINES}
        self.res = {}
        self.nsteps = 0
        self._blk = None
        self.pending = {e: [] for e in ENGINES}

    def _evt(self, eng):
        s = self.csem[eng]
        return (s._name, s.h, s.n, dict(self.clock[eng]), eng)

    @staticmethod
    def _root(k):
        return k.split(":", 1)[0] if ":" in k else None

    def _dep_evts(self, eng, reads, writes, keep_same_engine=False):
        evts = []
        def rd(k):
            r = self.res.get(k)
            if r and r["w"]:
                evts.append(r["w"])
        def wr(k):
            r = self.res.get(k)
            if r:
                if r["w"]:
                    evts.append(r["w"])
                evts.extend(r["r"])
        for k in reads:
            rd(k)
            ro = self._root(k)
            if ro:
                rd(ro)
        for k in writes:
            wr(k)
            ro = self._root(k)
            if ro:
                r = self.res.get(ro)
                if r:
                    if r["w"]:
                        evts.append(r["w"])
                    evts.extend(r["r"])
            else:
                pref = k + ":"
                for k2 in list(self.res):
                    if k2.startswith(pref):
                        wr(k2)
        out = []
        for ev in evts:
            (sn, sh, cnt, snap, seng) = ev
            if cnt <= 0:
                continue
            if seng == eng and not keep_same_engine:
                continue
            if seng == eng and sn == self.dsem[eng]._name:
                continue
            if self.clock[eng].get(sn, 0) >= cnt:
                continue
            out.append(ev)
        return out

    def _merge_waits(self, eng, evts):
        bysem = {}
        for ev in evts:
            sn = ev[0]
            if sn not in bysem or ev[2] > bysem[sn][2]:
                bysem[sn] = ev
        waits = []
        for (sn, sh, cnt, snap, seng) in bysem.values():
            waits.append((sh, cnt))
            c = self.clock[eng]
            c[sn] = max(c.get(sn, 0), cnt)
            for k, v in snap.items():
                c[k] = max(c.get(k, 0), v)
        return waits

    def block(self):
        self._blk = self.nc.Block()
        return self._blk

    def step(self, eng, emit, reads=(), writes=(), collective=False, fence=False,
             qdma=False):
        reads = tuple(reads); writes = tuple(writes)
        deps = self._dep_evts(eng, reads, writes, keep_same_engine=qdma)
        waits = self._merge_waits(eng, deps)
        self._cur_eng = eng
        self._cur_dmas = 0
        self._probe = True
        try:
            emit(_Probe())
        finally:
            self._probe = False
        ndma = self._cur_dmas
        assert not (eng == "gpsimd" and ndma and not fence), "gpsimd dma needs fence"
        if ndma:
            self.dsem[eng].n += 16 * ndma
        if collective:
            self.cc.n += 1
            evt = (self.cc._name, self.cc.h, self.cc.n, dict(self.clock[eng]), "-cc-")
            inc = ("cc", self.cc.h, self.cc.n)
        elif ndma and eng in ("sync", "scalar") and not fence:
            s = self.dsem[eng]
            evt = (s._name, s.h, s.n, dict(self.clock[eng]), eng + "-q")
            inc = None
        elif ndma and fence:
            s = self.dsem[eng]
            self.clock[eng][s._name] = s.n
            self.csem[eng].n += 1
            evt = self._evt(eng)
            inc = ("fence", s.h, s.n, self.csem[eng].h, self.csem[eng].n)
        else:
            self.csem[eng].n += 1
            evt = self._evt(eng)
            inc = ("eng", self.csem[eng].h, self.csem[eng].n)
        self.pending[eng].append((emit, waits, inc, ndma))
        for k in writes:
            self.res[k] = {"w": evt, "r": []}
        for k in reads:
            self.res.setdefault(k, {"w": None, "r": []})["r"].append(evt)
        self.nsteps += 1

    def flush(self):
        tr = self
        for eng in ENGINES:
            steps = self.pending[eng]
            if not steps:
                continue
            self.pending[eng] = []
            def body(e, steps=steps, eng=eng):
                for (emit, waits, inc, ndma) in steps:
                    for (sh, cnt) in waits:
                        e.wait_ge(sh, cnt)
                    tr._cur_eng = eng
                    tr._cur_dmas = 0
                    last = emit(e)
                    assert tr._cur_dmas == ndma, \
                        f"{eng} step declared ndma={ndma} emitted {tr._cur_dmas}"
                    if inc is None:
                        continue
                    if inc[0] == "cc":
                        last.then_inc(inc[1], 1)
                    elif inc[0] == "fence":
                        e.wait_ge(inc[1], inc[2])
                        e.nop().then_inc(inc[3], 1)
                    else:
                        if last is None:
                            last = e.nop()
                        last.then_inc(inc[1], 1)
            getattr(self._blk, eng)(body)

    def dma(self, e, out, in_, **kw):
        self._cur_dmas += 1
        if getattr(self, "_probe", False):
            return _ProbeInst()
        s = self.dsem[self._cur_eng]
        return e.dma_start(out=out, in_=in_, **kw).then_inc(s.h, 16)

    def idma(self, e, out, in_, in_offset):
        self._cur_dmas += 1
        if getattr(self, "_probe", False):
            return _ProbeInst()
        s = self.dsem[self._cur_eng]
        return e.indirect_dma_start(out, None, in_, in_offset).then_inc(s.h, 16)

    def barrier(self, regions):
        for rg in regions:
            keys = [k for k in self.res if k == rg or k.startswith(rg + ":")]
            self.step("vector", lambda e: None, reads=keys, writes=[rg])


def host_prep(inp):
    asc = np.ascontiguousarray
    w1 = np.asarray(inp["L0_ffn_w1"]); w2 = np.asarray(inp["L0_ffn_w2"])
    win = np.asarray(inp["L0_attn_W_in"]); wout = np.asarray(inp["L0_attn_W_out"])
    wx = np.asarray(inp["L0_attn_W_x"]); x = np.asarray(inp["L0_attn_x"])
    common = {
        "w1T": asc(w1.T), "w1n": asc(w1), "w2T": asc(w2.T), "w2n": asc(w2),
        "winT": asc(win.T), "woutT": asc(wout.T), "wxT": asc(wx.T),
        "b1L": asc(np.asarray(inp["L0_ffn_b1"]).reshape(HC, 128).T),
        "b2L": asc(np.asarray(inp["L0_ffn_b2"]).reshape(DMC, 128).T),
        "convbL": asc(np.asarray(inp["L0_attn_conv_b"]).reshape(DIC, 128).T),
        "DLn": asc(np.asarray(inp["L0_attn_D"]).reshape(DIC, 128).T),
        "gL": asc(np.asarray(inp["L0_attn_g"]).reshape(DMC, 128).T),
        "wdtL": asc(np.asarray(inp["L0_attn_W_dt"])[:, 0].reshape(DIC, 128).T),
        "bdtL": asc(np.asarray(inp["L0_attn_b_dt"]).reshape(DIC, 128).T),
        "convwL": asc(np.asarray(inp["L0_attn_conv_w"]).reshape(DIC, 128, 4)
                      .transpose(1, 0, 2).reshape(128, DIC * 4)),
        "eyeB": np.eye(128, dtype=ml_dtypes.bfloat16),
        "eyeF": np.eye(128, dtype=np.float32),
    }
    alog = np.asarray(inp["L0_attn_A_log"])
    wdt = np.asarray(inp["L0_attn_W_dt"])[:, 0]
    bdt = np.asarray(inp["L0_attn_b_dt"])
    p_ = np.arange(128)
    per_core = []
    for c in range(8):
        q = c % 4
        d = dict(common)
        d["xA"] = asc(x[0 + c // 4, q * TQ:(q + 1) * TQ, :])
        d["xB"] = asc(x[2 + c // 4, q * TQ:(q + 1) * TQ, :])
        acs = np.zeros((128, 4 * S), np.float32)
        wcs = np.zeros((128, 4), np.float32)
        bcs = np.zeros((128, 4), np.float32)
        for cci in range(4):
            gt = q * 4 + cci
            acs[:, cci * S:(cci + 1) * S] = alog[gt * 128:(gt + 1) * 128, :]
            wcs[:, cci] = wdt[gt * 128:(gt + 1) * 128]
            bcs[:, cci] = bdt[gt * 128:(gt + 1) * 128]
        d["alog_cs"] = acs; d["wdt_cs"] = wcs; d["bdt_cs"] = bcs
        d["mask3"] = np.full((128, 1), 0.0 if q == 0 else 1.0, np.float32)
        im = np.zeros((128, 16), np.uint32)
        iy = np.zeros((128, 16), np.uint32)
        it = np.zeros((128, 16), np.uint32)
        for col in range(16):
            cci, src = col // 4, col % 4
            im[:, col] = src * DI + q * 512 + cci * 128 + p_
        for t in range(16):
            iy[:, t] = (t // 4) * DI + ((t % 4) * 128 + p_) * 4 + q
            it[:, t] = ((q + 3) % 4) * DI + t * 128 + p_
        d["idx_mu"] = im; d["idx_y"] = iy; d["idx_tail"] = it
        per_core.append(d)
    return per_core


def build(dbg_stage=0):
    nc = bass.Bass("TRN2", target_bir_lowering=False)
    din = {}
    def DIN(name, shape, dt=F32):
        din[name] = nc.dram_tensor(name, shape, dt, kind="ExternalInput").ap()
    DIN("xA", [TQ, DM]); DIN("xB", [TQ, DM])
    DIN("w1T", [DM, H]); DIN("w1n", [H, DM])
    DIN("w2T", [H, DM]); DIN("w2n", [DM, H])
    DIN("winT", [DM, H]); DIN("woutT", [DI, DM]); DIN("wxT", [DI, 33])
    for nm, w_ in [("b1L", HC), ("b2L", DMC), ("convbL", DIC), ("DLn", DIC),
                   ("gL", DMC), ("wdtL", DIC), ("bdtL", DIC), ("convwL", 4 * DIC),
                   ("alog_cs", 4 * S), ("wdt_cs", 4), ("bdt_cs", 4), ("mask3", 1)]:
        DIN(nm, [128, w_])
    DIN("eyeB", [128, 128], BF16); DIN("eyeF", [128, 128], F32)
    DIN("idx_mu", [128, 16], U32); DIN("idx_y", [128, 16], U32)
    DIN("idx_tail", [128, 16], U32)

    outs_w = [nc.dram_tensor("outA", [TQ, DM], F32, kind="ExternalOutput").ap(),
              nc.dram_tensor("outB", [TQ, DM], F32, kind="ExternalOutput").ap()]
    if dbg_stage:
        dbgF = nc.dram_tensor("dbgF", [128, 16448], F32, kind="ExternalOutput").ap()
        dbgB = nc.dram_tensor("dbgB", [128, 33000], BF16, kind="ExternalOutput").ap()

    w1s_d = nc.dram_tensor("w1s_d", [DMC, 128, H], FP8).ap()
    w2s_dh = nc.dram_tensor("w2s_d", [HC, 128, DM], FP8)
    w2s_d = w2s_dh.ap()
    xc_sp = nc.dram_tensor("xc_sp", [DIC, 128, TQ], BF16).ap()
    sz_sp = nc.dram_tensor("sz_sp", [DIC, 128, TQ], BF16).ap()
    xfm_sp = nc.dram_tensor("xfm_sp", [DMC, 128, TQ], F32).ap()
    strow_dh = nc.dram_tensor("strow_d", [4, 1024], F32)
    strow_d = strow_dh.ap()
    Bc_dh = nc.dram_tensor("Bc_d", [16, T], BF16)
    Cc_dh = nc.dram_tensor("Cc_d", [16, T], BF16)
    Bc_d = Bc_dh.ap(); Cc_d = Cc_dh.ap()
    tail_in = [nc.dram_tensor(f"tail_in{w}", [DI, 4], BF16).ap() for w in range(NW)]
    tail_out = [nc.dram_tensor(f"tail_out{w}", [4 * DI, 4], BF16).ap() for w in range(NW)]
    mu_in = [nc.dram_tensor(f"mu_in{w}", [DI, TQ], BF16).ap() for w in range(NW)]
    mu_out = [nc.dram_tensor(f"mu_out{w}", [4 * DI, TQ], BF16).ap() for w in range(NW)]
    pr_in_h = [nc.dram_tensor(f"pr_in{w}", [33, TQ], F32) for w in range(NW)]
    pr_in = [h.ap() for h in pr_in_h]
    pr_out_h = [nc.dram_tensor(f"pr_out{w}", [132, TQ], F32) for w in range(NW)]
    pr_out = [h.ap() for h in pr_out_h]
    y_in_h = [nc.dram_tensor(f"y_in{w}", [2048, TQ], BF16) for w in range(NW)]
    y_in = [h.ap() for h in y_in_h]
    y_out = [nc.dram_tensor(f"y_out{w}", [8192, TQ], BF16).ap() for w in range(NW)]

    A_ = nc.alloc_sbuf_tensor
    eyeB = A_("eyeB_s", [128, 128], BF16); eyeF = A_("eyeF_s", [128, 128], F32)
    negeye = A_("negeye_s", [128, 128], BF16)
    lan = {}
    for nm, w_ in [("b1L", HC), ("b2L", DMC), ("convbL", DIC), ("DLn", DIC),
                   ("gL", DMC), ("wdtL", DIC), ("bdtL", DIC), ("convwL", 4 * DIC),
                   ("alog_cs", 4 * S), ("wdt_cs", 4), ("bdt_cs", 4), ("mask3", 1)]:
        lan[nm] = A_(nm + "_s", [128, w_], F32)
    alpha1L = A_("alpha1L", [128, HC], F32); alpha2L = A_("alpha2L", [128, DMC], F32)
    b2hL = A_("b2hL", [128, DMC], F32)
    negwdtL = A_("negwdtL", [128, DIC], F32); negbdtL = A_("negbdtL", [128, DIC], F32)
    negwdt_cs = A_("negwdt_cs", [128, 4], F32); negbdt_cs = A_("negbdt_cs", [128, 4], F32)
    negA_cs = A_("negA_cs", [128, 4 * S], F32)
    onesL = A_("onesL", [128, 1], F32)
    epsP = A_("epsP", [128, 2], F32)
    idx_mu = A_("idx_mu_s", [128, 16], U32); idx_y = A_("idx_y_s", [128, 16], U32)
    idx_tail = A_("idx_tail_s", [128, 16], U32)
    state = A_("state_s", [128, 4 * S], F32)
    wxTs = A_("wxTs", [128, DIC * 33], BF16)

    KB = 1024
    ARENA = 193 * KB
    base = nc.sbuf_base
    arena_slab = nc.alloc_sbuf_tensor("arena_slab", [128, ARENA], mybir.dt.uint8)
    def place(name, shape, dt, off_b):
        assert off_b < ARENA
        return nc.alloc_sbuf_tensor_at(name, shape, dt, offset=base + off_b)
    x_fm   = place("x_fm",   [128, 8 * TQ], F32, 0)
    t1b    = place("t1b",    [128, 4096], F32, 0)
    bbuf   = place("bbuf",   [128, 2 * T], BF16, 0)
    hbuf   = place("hbuf",   [128, 2 * T], BF16, 16 * KB)
    xn     = place("xn",     [128, 8 * TQ], BF16, 32 * KB)
    xnf    = place("xnf",    [128, 4 * TQ], F32, 113 * KB)
    Bsb    = place("Bsb",    [16, T], BF16, 32 * KB)
    Csb    = place("Csb",    [16, T], BF16, 40 * KB)
    wbig   = place("wbig",   [128, 16384], BF16, 48 * KB)
    wbigF  = place("wbigF",  [128, 8192], F32, 48 * KB)
    xsp    = place("xsp",    [128, DIC * XSW], BF16, 80 * KB)
    mu_ch  = place("mu_ch",  [128, 4 * T], BF16, 80 * KB)
    gelu_g = place("gelu_g", [128, HC * 512], BF16, 113 * KB)
    repsF  = place("repsF",  [128, 4 * T], BF16, 113 * KB)
    w1s8   = place("w1s8",   [128, DMC * H], FP8, 145 * KB)
    diagc  = place("diagc",  [128, 64 * 128], BF16, 145 * KB)
    cgate  = place("cgate",  [128, 8 * TQ], BF16, 145 * KB)
    abuf   = place("abuf",   [128, 2 * T], F32, 145 * KB)
    ystg   = place("ystg",   [128, 2 * T], BF16, 32 * KB)
    f32tmp = place("f32tmp", [128, 4096], F32, 177 * KB)
    w2s8   = place("w2s8",   [128, 4 * 1024], FP8, 189 * KB)
    prow   = place("prow",   [33, TQ], F32, 189 * KB)
    st6    = place("st6",    [1, 3072], F32, 189 * KB)
    g8 = gelu_g.bitcast(FP8)

    aps1 = nc.alloc_psum_tensor("aps1", [128, 1024], F32)
    aps2 = nc.alloc_psum_tensor("aps2", [128, 1024], F32)
    yps = nc.alloc_psum_tensor("yps", [128, 1024], F32)
    pbA = nc.alloc_psum_tensor("pbA", [128, 512], F32)
    pbB = nc.alloc_psum_tensor("pbB", [128, 512], F32)
    BANK8 = [pbA[:, :], pbB[:, :], aps1[:, 0:512], aps1[:, 512:1024],
             aps2[:, 0:512], aps2[:, 512:1024], yps[:, 0:512], yps[:, 512:1024]]
    BKEY = ["P:a", "P:b", "P:1l", "P:1h", "P:2l", "P:2h", "P:yl", "P:yh"]

    tr = Tracker(nc)
    RG = [[0, 1, 2, 3], [4, 5, 6, 7]]

    def dump(items_f32=(), items_bf16=(), extra_reads=()):
        def d(e):
            for (t_, c0, c1, dc) in items_f32:
                tr.dma(e, dbgF[0:t_.shape[0], dc:dc + (c1 - c0)], t_[0:t_.shape[0], c0:c1])
            for (t_, c0, c1, dc) in items_bf16:
                tr.dma(e, dbgB[0:t_.shape[0], dc:dc + (c1 - c0)], t_[0:t_.shape[0], c0:c1])
            return None
        tr.step("sync", d, reads=list(extra_reads) +
                [k for k in tr.res if k.startswith(("R0", "R1", "R2", "R3", "R4",
                                                    "R5", "R7"))])

    with tr.block():
        def ld_consts(e):
            for nm in lan:
                tr.dma(e, lan[nm][:, :], din[nm])
            for nm, t_ in [("eyeB", eyeB), ("eyeF", eyeF), ("idx_mu", idx_mu),
                           ("idx_y", idx_y), ("idx_tail", idx_tail)]:
                tr.dma(e, t_[:, :], din[nm])
            return None
        tr.step("sync", ld_consts, writes=["consts"])

        def ld_wxT(e):
            for t_ in range(DIC):
                tr.dma(e, wxTs[:, t_ * 33:(t_ + 1) * 33],
                       din["wxT"][t_ * 128:(t_ + 1) * 128, :])
            return None
        tr.step("gpsimd", ld_wxT, writes=["wxTs"], fence=True)

        def v_consts(e):
            e.memset(onesL[:, :], 1.0)
            e.memset(epsP[:, 0:1], 1e-5)
            e.memset(epsP[:, 1:2], 1e-6)
            e.tensor_scalar(negwdtL[:, :], lan["wdtL"][:, :], -1.0, None, ALU.mult)
            e.tensor_scalar(negbdtL[:, :], lan["bdtL"][:, :], -1.0, None, ALU.mult)
            e.tensor_scalar(negwdt_cs[:, :], lan["wdt_cs"][:, :], -1.0, None, ALU.mult)
            e.tensor_scalar(negbdt_cs[:, :], lan["bdt_cs"][:, :], -1.0, None, ALU.mult)
            e.tensor_scalar(negeye[:, :], eyeB[:, :], -1.0, None, ALU.mult)
            return e.tensor_scalar(b2hL[:, :], lan["b2L"][:, :], 0.5, None, ALU.mult)
        tr.step("vector", v_consts, reads=["consts"], writes=["vconsts"])
        tr.step("scalar",
                lambda e: e.activation(negA_cs[:, :], lan["alog_cs"][:, :], AF.Exp),
                reads=["consts"], writes=["negA"])

        for half in range(2):
            ks = list(range(half * 4, half * 4 + 4))
            def ld_w1(e, ks=ks):
                for k in ks:
                    tr.dma(e, wbig[:, (k % 4) * 4096:(k % 4) * 4096 + 4096],
                           din["w1T"][k * 128:(k + 1) * 128, :])
                return None
            tr.step("gpsimd", ld_w1, writes=["R2"], fence=True)
            def sg_w1(e, ks=ks):
                for k in ks:
                    r = e.activation(w1s8[:, k * 4096:(k + 1) * 4096],
                                     wbig[:, (k % 4) * 4096:(k % 4) * 4096 + 4096],
                                     AF.Sign)
                return r
            tr.step("scalar", sg_w1, reads=["R2"], writes=["R5"])
        def st_w1(e):
            for k in range(DMC):
                tr.dma(e, w1s_d[k, :, :], w1s8[:, k * 4096:(k + 1) * 4096])
            return None
        tr.step("sync", st_w1, reads=["R5"], writes=["w1s_d"])

        for hh in range(HC):
            sl = (hh % 16) * 1024
            tr.step("gpsimd",
                    lambda e, hh=hh, sl=sl: tr.dma(
                        e, wbig[:, sl:sl + 1024], din["w2T"][hh * 128:(hh + 1) * 128, :]),
                    writes=[f"R2:w2_{hh % 16}"], fence=True)
            tr.step("scalar",
                    lambda e, hh=hh, sl=sl: e.activation(
                        g8[:, hh * 1024:(hh + 1) * 1024], wbig[:, sl:sl + 1024], AF.Sign),
                    reads=[f"R2:w2_{hh % 16}"], writes=[f"R4:w2s_{hh}"])
            tr.step("sync",
                    lambda e, hh=hh: tr.dma(e, w2s_d[hh, :, :],
                                            g8[:, hh * 1024:(hh + 1) * 1024]),
                    reads=[f"R4:w2s_{hh}"], writes=[f"w2s_d:{hh // 4}_{hh % 4}"])

        for t_ in range(HC):
            sl = (t_ % 2) * 1024
            tr.step("gpsimd",
                    lambda e, t_=t_, sl=sl: tr.dma(
                        e, xsp[:, sl:sl + 1024], din["w1n"][t_ * 128:(t_ + 1) * 128, :]),
                    writes=[f"R3:a{t_ % 2}"], fence=True)
            tr.step("scalar",
                    lambda e, t_=t_, sl=sl: e.activation(
                        xn[:, 0:1024], xsp[:, sl:sl + 1024], AF.Abs,
                        accum_out=alpha1L[:, t_:t_ + 1]),
                    reads=[f"R3:a{t_ % 2}"], writes=["R1:abs", f"al1:{t_}"])
        for t_ in range(DMC):
            sl = (t_ % 2) * 4096
            tr.step("gpsimd",
                    lambda e, t_=t_, sl=sl: tr.dma(
                        e, xsp[:, sl:sl + 4096], din["w2n"][t_ * 128:(t_ + 1) * 128, :]),
                    writes=[f"R3:b{t_ % 2}"], fence=True)
            tr.step("scalar",
                    lambda e, t_=t_, sl=sl: e.activation(
                        xn[:, 0:4096], xsp[:, sl:sl + 4096], AF.Abs,
                        accum_out=alpha2L[:, t_:t_ + 1]),
                    reads=[f"R3:b{t_ % 2}"], writes=["R1:abs", f"al2:{t_}"])
        def v_alpha(e):
            e.tensor_scalar(alpha1L[:, :], alpha1L[:, :], 1.0 / DM, None, ALU.mult)
            return e.tensor_scalar(alpha2L[:, :], alpha2L[:, :], 0.5 / H, None, ALU.mult)
        tr.step("vector", v_alpha,
                reads=[f"al1:{t_}" for t_ in range(HC)] +
                      [f"al2:{t_}" for t_ in range(DMC)],
                writes=["alphas"])
        tr.flush()

    def emit_LN(eps, rms_after=False):
        for k in range(DMC):
            tr.step("scalar",
                    lambda e, k=k: e.activation(f32tmp[:, 0:1024],
                                                x_fm[:, k * 1024:(k + 1) * 1024],
                                                AF.Square),
                    reads=[f"R0:x{k}"], writes=["R7:q0"])
            def mms(e, k=k):
                st = (k == 0); sp = (k == DMC - 1)
                e.matmul(pbA[0:1, :], onesL[:, :], x_fm[:, k * 1024:k * 1024 + 512],
                         start=st, stop=sp)
                e.matmul(pbB[0:1, :], onesL[:, :],
                         x_fm[:, k * 1024 + 512:(k + 1) * 1024], start=st, stop=sp)
                e.matmul(aps1[0:1, 0:512], onesL[:, :], f32tmp[:, 0:512],
                         start=st, stop=sp)
                return e.matmul(aps1[0:1, 512:1024], onesL[:, :], f32tmp[:, 512:1024],
                                start=st, stop=sp)
            tr.step("tensor", mms, reads=["R7:q0", f"R0:x{k}", "vconsts"],
                    writes=["P:a", "P:b", "P:1l", "P:1h"])
        def v_stats(e):
            e.tensor_scalar(st6[0:1, 0:512], pbA[0:1, :], 1.0 / DM, None, ALU.mult)
            e.tensor_scalar(st6[0:1, 512:1024], pbB[0:1, :], 1.0 / DM, None, ALU.mult)
            e.tensor_scalar(st6[0:1, 1024:1536], aps1[0:1, 0:512], 1.0 / DM, None,
                            ALU.mult)
            e.tensor_scalar(st6[0:1, 1536:2048], aps1[0:1, 512:1024], 1.0 / DM, None,
                            ALU.mult)
            e.tensor_tensor(st6[0:1, 2048:3072], st6[0:1, 0:1024], st6[0:1, 0:1024],
                            ALU.mult)
            return e.tensor_tensor(st6[0:1, 1024:2048], st6[0:1, 1024:2048],
                                   st6[0:1, 2048:3072], ALU.subtract)
        tr.step("vector", v_stats, reads=["P:a", "P:b", "P:1l", "P:1h"],
                writes=["R7:q3_0", "R7:q3_1", "R7:q3_2"])
        tr.step("scalar",
                lambda e: e.activation(st6[0:1, 1024:2048], st6[0:1, 1024:2048],
                                       AF.Sqrt, bias=epsP[0:1, 0:1]),
                reads=["R7:q3_1", "vconsts"], writes=["R7:q3_1"])
        def v_rinv(e):
            e.reciprocal(st6[0:1, 2048:3072], st6[0:1, 1024:2048])
            e.tensor_copy(st6[0:1, 1024:2048], st6[0:1, 2048:3072])
            return e.tensor_tensor(st6[0:1, 2048:3072], st6[0:1, 0:1024],
                                   st6[0:1, 2048:3072], ALU.mult)
        tr.step("vector", v_rinv, reads=["R7:q3_1", "R7:q3_0"],
                writes=["R7:q3_1", "R7:q3_2"])
        def bc0(e):
            tr.dma(e, strow_d[0:1, :], st6[0:1, 1024:2048])
            tr.dma(e, strow_d[1:2, :], st6[0:1, 2048:3072])
            return None
        tr.step("sync", bc0, reads=["R7:q3_1", "R7:q3_2"], writes=["strow"])
        def bc(e):
            tr.dma(e, f32tmp[:, 1024:2048],
                   bass.AP(strow_dh, 0, [[1, 1], [0, 128], [1, 1024]]))
            tr.dma(e, f32tmp[:, 2048:3072],
                   bass.AP(strow_dh, 1024, [[1, 1], [0, 128], [1, 1024]]))
            return None
        tr.step("sync", bc, reads=["strow"], writes=["R7:q1", "R7:q2"])

        if not rms_after:
            for k in range(DMC):
                def v_app(e, k=k):
                    sl = (k % 4) * 1024
                    e.tensor_tensor(xnf[:, sl:sl + 1024],
                                    x_fm[:, k * 1024:(k + 1) * 1024],
                                    f32tmp[:, 1024:2048], ALU.mult)
                    return e.tensor_tensor(xn[:, k * 1024:(k + 1) * 1024],
                                           xnf[:, sl:sl + 1024],
                                           f32tmp[:, 2048:3072], ALU.subtract)
                j = k % 4
                tr.step("vector", v_app, reads=["R7:q1", "R7:q2", f"R0:x{k}"],
                        writes=[f"R1:n{k}", f"R4:g{2*j}", f"R4:g{2*j+1}"])
        else:
            for k in range(DMC):
                def v_app2(e, k=k):
                    sl = (k % 4) * 1024
                    e.tensor_tensor(xnf[:, sl:sl + 1024],
                                    x_fm[:, k * 1024:(k + 1) * 1024],
                                    f32tmp[:, 1024:2048], ALU.mult)
                    e.tensor_tensor(xnf[:, sl:sl + 1024], xnf[:, sl:sl + 1024],
                                    f32tmp[:, 2048:3072], ALU.subtract)
                    e.tensor_copy(xn[:, k * 1024:(k + 1) * 1024], xnf[:, sl:sl + 1024])
                    return e.tensor_tensor(x_fm[:, k * 1024:(k + 1) * 1024],
                                           x_fm[:, k * 1024:(k + 1) * 1024],
                                           xnf[:, sl:sl + 1024], ALU.add)
                j = k % 4
                tr.step("vector", v_app2, reads=["R7:q1", "R7:q2"],
                        writes=[f"R0:x{k}", f"R1:n{k}", f"R4:g{2*j}", f"R4:g{2*j+1}"])
                tr.step("scalar",
                        lambda e, k=k: e.activation(
                            f32tmp[:, 0:1024],
                            xnf[:, (k % 4) * 1024:(k % 4) * 1024 + 1024], AF.Square),
                        reads=[f"R4:g{2*(k%4)}", f"R4:g{2*(k%4)+1}"], writes=["R7:q0"])
                def mms2(e, k=k):
                    st = (k == 0); sp = (k == DMC - 1)
                    e.matmul(pbA[0:1, :], onesL[:, :], f32tmp[:, 0:512], start=st, stop=sp)
                    return e.matmul(pbB[0:1, :], onesL[:, :], f32tmp[:, 512:1024],
                                    start=st, stop=sp)
                tr.step("tensor", mms2, reads=["R7:q0", "vconsts"], writes=["P:a", "P:b"])
            def v_ms(e):
                e.tensor_scalar(st6[0:1, 0:512], pbA[0:1, :], 1.0 / DM, None, ALU.mult)
                return e.tensor_scalar(st6[0:1, 512:1024], pbB[0:1, :], 1.0 / DM, None,
                                       ALU.mult)
            tr.step("vector", v_ms, reads=["P:a", "P:b"], writes=["R7:q3_0"])
            tr.step("scalar",
                    lambda e: e.activation(st6[0:1, 0:1024], st6[0:1, 0:1024], AF.Sqrt,
                                           bias=epsP[0:1, 1:2]),
                    reads=["R7:q3_0", "vconsts"], writes=["R7:q3_0"])
            tr.step("vector",
                    lambda e: e.reciprocal(st6[0:1, 1024:2048], st6[0:1, 0:1024]),
                    reads=["R7:q3_0"], writes=["R7:q3_1"])
            tr.step("sync",
                    lambda e: tr.dma(e, strow_d[2:3, :], st6[0:1, 1024:2048]),
                    reads=["R7:q3_1"], writes=["strow"])
            tr.step("sync",
                    lambda e: tr.dma(e, f32tmp[:, 1024:2048],
                                     bass.AP(strow_dh, 2048, [[1, 1], [0, 128], [1, 1024]])),
                    reads=["strow"], writes=["R7:q1"])
            for k in range(DMC):
                tr.step("vector",
                        lambda e, k=k: e.scalar_tensor_tensor(
                            xn[:, k * 1024:(k + 1) * 1024],
                            xn[:, k * 1024:(k + 1) * 1024],
                            lan["gL"][:, k:k + 1], f32tmp[:, 1024:2048],
                            ALU.mult, ALU.mult),
                        reads=["R7:q1", f"R1:n{k}", "consts"], writes=[f"R1:n{k}"])

    def emit_ffn(reload_w1):
        if reload_w1:
            def ld1(e):
                for k in range(DMC):
                    tr.dma(e, w1s8[:, k * 4096:(k + 1) * 4096], w1s_d[k, :, :])
                return None
            tr.step("sync", ld1, reads=["w1s_d"], writes=["R5"])
        for tc in range(2):
            for ht in range(HC):
                bank = [pbA, pbB][ht % 2]
                bkey = ["P:a", "P:b"][ht % 2]
                def mm1(e, ht=ht, tc=tc, bank=bank):
                    for k in range(DMC):
                        r = e.matmul(bank[:, :],
                                     w1s8[:, k * 4096 + ht * 128:k * 4096 + (ht + 1) * 128],
                                     xn[:, k * 1024 + tc * 512:k * 1024 + tc * 512 + 512],
                                     start=(k == 0), stop=(k == DMC - 1))
                    return r
                tr.step("tensor", mm1,
                        reads=["R5"] + [f"R1:n{k}" for k in range(DMC)], writes=[bkey])
                tr.step("scalar",
                        lambda e, ht=ht, bank=bank: e.activation(
                            gelu_g[:, ht * 512:(ht + 1) * 512], bank[:, :], AF.Gelu,
                            bias=lan["b1L"][:, ht:ht + 1], scale=alpha1L[:, ht:ht + 1]),
                        reads=[bkey, "alphas", "consts"], writes=[f"R4:g{ht}"])
            for hg in range(HC // 4):
                def ldw2(e, hg=hg):
                    tr.dma(e, w2s8[:, :],
                           bass.AP(w2s_dh, hg * 4 * 128 * DM,
                                   [[DM, 128], [128 * DM, 4], [1, DM]]))
                    return None
                tr.step("sync", ldw2,
                        reads=[f"w2s_d:{hg}_{i}" for i in range(4)],
                        writes=["R7:q3_0", "R7:q3_1", "R7:q3_2", "R7:q3_3"])
                def mm2(e, hg=hg):
                    for q4 in range(4):
                        hh = hg * 4 + q4
                        for d in range(DMC):
                            r = e.matmul(BANK8[d],
                                         w2s8[:, q4 * 1024 + d * 128:q4 * 1024 + (d + 1) * 128],
                                         gelu_g[:, hh * 512:(hh + 1) * 512],
                                         start=(hh == 0), stop=(hh == HC - 1))
                    return r
                tr.step("tensor", mm2,
                        reads=["R7:q3_0", "R7:q3_1", "R7:q3_2", "R7:q3_3"] +
                              [f"R4:g{hg * 4 + i}" for i in range(4)],
                        writes=BKEY)
            for d in range(DMC):
                def vev(e, d=d, tc=tc):
                    sl = 2048 + (d % 2) * 512
                    e.tensor_scalar(f32tmp[:, sl:sl + 512], BANK8[d],
                                    alpha2L[:, d:d + 1], b2hL[:, d:d + 1],
                                    ALU.mult, ALU.add)
                    dst = x_fm[:, d * 1024 + tc * 512:d * 1024 + tc * 512 + 512]
                    return e.tensor_tensor(dst, dst, f32tmp[:, sl:sl + 512], ALU.add)
                tr.step("vector", vev, reads=[BKEY[d], "alphas", f"R0:x{d}"],
                        writes=[f"R0:x{d}", "R7:q2"])

    def emit_wave(w):
        xin = din["xA"] if w == 0 else din["xB"]

        for tt in range(8):
            sl = (tt % 2) * 1024
            tr.step("sync",
                    lambda e, tt=tt, sl=sl: tr.dma(
                        e, f32tmp[:, sl:sl + 1024], xin[tt * 128:(tt + 1) * 128, :]),
                    writes=[f"R7:q{tt % 2}"])
            for k in range(DMC):
                bank = [aps1, aps2][k % 2]
                bkey = [("P:1l", "P:1h"), ("P:2l", "P:2h")][k % 2]
                tr.step("tensor",
                        lambda e, tt=tt, sl=sl, k=k, bank=bank: e.transpose(
                            bank[:, 0:128], f32tmp[:, sl + k * 128:sl + (k + 1) * 128],
                            eyeF[:, :]),
                        reads=[f"R7:q{tt % 2}", "consts"], writes=list(bkey))
                tr.step("scalar",
                        lambda e, tt=tt, k=k, bank=bank: e.activation(
                            x_fm[:, k * 1024 + tt * 128:k * 1024 + (tt + 1) * 128],
                            bank[:, 0:128], AF.Identity),
                        reads=list(bkey), writes=[f"R0:x{k}"])

        emit_LN(1e-5)
        if dbg_stage == 11:
            dump(items_f32=[(x_fm, 0, 8192, 0)], items_bf16=[(xn, 0, 8192, 0)])
            return
        emit_ffn(reload_w1=(w > 0))
        if dbg_stage == 12:
            dump(items_f32=[(x_fm, 0, 8192, 0)])
            return
        emit_LN(1e-5, rms_after=True)
        if dbg_stage == 13:
            dump(items_f32=[(x_fm, 0, 8192, 0)], items_bf16=[(xn, 0, 8192, 0)])
            return

        def v_diag(e):
            for t_ in range(DIC):
                for j in range(4):
                    r = e.tensor_scalar(
                        diagc[:, (t_ * 4 + j) * 128:(t_ * 4 + j + 1) * 128],
                        eyeB[:, :], lan["convwL"][:, t_ * 4 + j:t_ * 4 + j + 1],
                        None, ALU.mult)
            return r
        tr.step("vector", v_diag, reads=["consts"], writes=["R5:dg"])

        for half in range(2):
            def ldw(e, half=half):
                for k in range(DMC):
                    tr.dma(e, wbig[:, k * 2048:(k + 1) * 2048],
                           din["winT"][k * 128:(k + 1) * 128,
                                       half * 2048:(half + 1) * 2048])
                return None
            tr.step("gpsimd", ldw, writes=["R2"], fence=True)
            for oc in range(DIC):
                bank = [pbA, pbB][oc % 2]
                bkey = ["P:a", "P:b"][oc % 2]
                for tc in range(2):
                    def mmw(e, oc=oc, tc=tc, bank=bank):
                        for k in range(DMC):
                            r = e.matmul(
                                bank[:, :],
                                wbig[:, k * 2048 + oc * 128:k * 2048 + (oc + 1) * 128],
                                xn[:, k * 1024 + tc * 512:k * 1024 + tc * 512 + 512],
                                start=(k == 0), stop=(k == DMC - 1))
                        return r
                    tr.step("tensor", mmw,
                            reads=["R2"] + [f"R1:n{k}" for k in range(DMC)],
                            writes=[bkey])
                    if half == 0:
                        tr.step("scalar",
                                lambda e, oc=oc, tc=tc, bank=bank: e.activation(
                                    xsp[:, oc * XSW + 4 + tc * 512:
                                        oc * XSW + 4 + tc * 512 + 512],
                                    bank[:, :], AF.Identity),
                                reads=[bkey], writes=[f"R3:xs{oc}_{tc}"])
                    else:
                        tr.step("scalar",
                                lambda e, oc=oc, tc=tc, bank=bank: e.activation(
                                    gelu_g[:, oc * 1024 + tc * 512:
                                           oc * 1024 + tc * 512 + 512],
                                    bank[:, :], AF.Silu),
                                reads=[bkey], writes=[f"R4:g{2*oc+tc}"])
                if half == 1:
                    tr.step("scalar",
                            lambda e, oc=oc: tr.dma(
                                e, sz_sp[oc, :, :], gelu_g[:, oc * 1024:(oc + 1) * 1024]),
                            reads=[f"R4:g{2*oc}", f"R4:g{2*oc+1}"], writes=[f"szsp:{oc}"],
                            qdma=True)

        def tails(e):
            for t_ in range(DIC):
                tr.dma(e, tail_in[w][t_ * 128:(t_ + 1) * 128, :],
                       xsp[:, t_ * XSW + 4 + 1020:t_ * XSW + 4 + 1024])
            return None
        tr.step("sync", tails, reads=[f"R3:xs{t_}_1" for t_ in range(DIC)],
                writes=["tin"])
        tr.step("gpsimd",
                lambda e: e.collective_compute(
                    "AllGather", ALU.bypass, replica_groups=RG,
                    ins=[tail_in[w].opt()], outs=[tail_out[w].opt()]),
                reads=["tin"], writes=["tout"], collective=True)
        def halo(e):
            for t_ in range(DIC):
                tr.idma(e, xsp[:, t_ * XSW:t_ * XSW + 4], tail_out[w],
                        IOff(idx_tail[:, t_:t_ + 1], 0))
            return None
        tr.step("gpsimd", halo, reads=["tout", "consts"],
                writes=[f"R3:h{t_}" for t_ in range(DIC)], fence=True)
        def vmask(e):
            for t_ in range(DIC):
                r = e.tensor_scalar(xsp[:, t_ * XSW:t_ * XSW + 4],
                                    xsp[:, t_ * XSW:t_ * XSW + 4],
                                    lan["mask3"][:, 0:1], None, ALU.mult)
            return r
        tr.step("vector", vmask,
                reads=[f"R3:h{t_}" for t_ in range(DIC)] + ["consts"], writes=["R3:hm"])

        for t_ in range(DIC):
            bank = [pbA, pbB][t_ % 2]
            bkey = ["P:a", "P:b"][t_ % 2]
            for tc in range(2):
                def mmc(e, t_=t_, tc=tc, bank=bank):
                    for j in range(4):
                        r = e.matmul(
                            bank[:, :],
                            diagc[:, (t_ * 4 + j) * 128:(t_ * 4 + j + 1) * 128],
                            xsp[:, t_ * XSW + 1 + j + tc * 512:
                                t_ * XSW + 1 + j + tc * 512 + 512],
                            start=(j == 0), stop=(j == 3))
                    return r
                tr.step("tensor", mmc,
                        reads=["R5:dg", "R3:hm", f"R3:xs{t_}_0", f"R3:xs{t_}_1"],
                        writes=[bkey])
                tr.step("scalar",
                        lambda e, t_=t_, tc=tc, bank=bank: e.activation(
                            gelu_g[:, t_ * 1024 + tc * 512:t_ * 1024 + tc * 512 + 512],
                            bank[:, :], AF.Silu, bias=lan["convbL"][:, t_:t_ + 1]),
                        reads=[bkey, "consts", f"szsp:{t_}"],
                        writes=[f"R4:g{2*t_+tc}"])
            tr.step("scalar",
                    lambda e, t_=t_: tr.dma(e, xc_sp[t_, :, :],
                                            gelu_g[:, t_ * 1024:(t_ + 1) * 1024]),
                    reads=[f"R4:g{2*t_}", f"R4:g{2*t_+1}"], writes=[f"xcsp:{t_}"],
                    qdma=True)

        for tc in range(2):
            bank = [pbA, pbB][tc]
            bkey = ["P:a", "P:b"][tc]
            def mmx(e, tc=tc, bank=bank):
                for t_ in range(DIC):
                    r = e.matmul(bank[0:33, :], wxTs[:, t_ * 33:(t_ + 1) * 33],
                                 gelu_g[:, t_ * 1024 + tc * 512:t_ * 1024 + tc * 512 + 512],
                                 start=(t_ == 0), stop=(t_ == DIC - 1))
                return r
            tr.step("tensor", mmx,
                    reads=["wxTs"] + [f"R4:g{2*t_+tc}" for t_ in range(DIC)],
                    writes=[bkey])
            tr.step("scalar",
                    lambda e, tc=tc, bank=bank: e.activation(
                        prow[:, tc * 512:(tc + 1) * 512], bank[0:33, :], AF.Identity),
                    reads=[bkey], writes=[f"R7:q3_{tc}", f"prow{tc}"])
        tr.step("sync", lambda e: tr.dma(e, pr_in[w][:, :], prow[:, :]),
                reads=["prow0", "prow1"], writes=["prin"])

        tr.step("sync",
                lambda e, w=w: tr.dma(e, f32tmp[:, 0:1024],
                                 bass.AP(pr_in_h[w], 32 * TQ, [[1, 1], [0, 128], [1, 1024]])),
                reads=["prin"], writes=["R7:q0"])
        for g in range(4):
            ts = [g * 4 + i for i in range(4)]
            def sig4(e, ts=ts):
                for i in range(4):
                    r = e.activation(wbigF[:, i * 1024:(i + 1) * 1024], f32tmp[:, 0:1024],
                                     AF.Sigmoid, scale=negwdtL[:, ts[i]:ts[i] + 1],
                                     bias=negbdtL[:, ts[i]:ts[i] + 1])
                return r
            tr.step("scalar", sig4, reads=["R7:q0", "vconsts"],
                    writes=[f"R2:f{i}" for i in range(4)])
            for i, t_ in enumerate(ts):
                tr.step("scalar",
                        lambda e, i=i: e.activation(f32tmp[:, 1024:2048],
                                                    wbigF[:, i * 1024:(i + 1) * 1024],
                                                    AF.Ln),
                        reads=[f"R2:f{i}"], writes=["R7:q1"])
                tr.step("vector",
                        lambda e, t_=t_: e.tensor_tensor(
                            f32tmp[:, 2048:2560].bitcast(BF16), f32tmp[:, 1024:2048],
                            gelu_g[:, t_ * 1024:(t_ + 1) * 1024], ALU.mult),
                        reads=["R7:q1", f"R4:g{2*t_}", f"R4:g{2*t_+1}"],
                        writes=["R7:q2"])
                tr.step("sync",
                        lambda e, t_=t_: tr.dma(e, mu_in[w][t_ * 128:(t_ + 1) * 128, :],
                                                f32tmp[:, 2048:2560].bitcast(BF16)),
                        reads=["R7:q2"], writes=[f"muin:{t_}"])

        def spx(e):
            for k in range(DMC):
                tr.dma(e, xfm_sp[k, :, :], x_fm[:, k * 1024:(k + 1) * 1024])
            return None
        tr.step("scalar", spx, reads=[f"R0:x{k}" for k in range(DMC)], writes=["xfmsp"],
                qdma=True)

        tr.step("gpsimd",
                lambda e: e.collective_compute(
                    "AllGather", ALU.bypass, replica_groups=RG,
                    ins=[mu_in[w].opt()], outs=[mu_out[w].opt()]),
                reads=[f"muin:{t_}" for t_ in range(DIC)], writes=["muout"],
                collective=True)
        tr.step("gpsimd",
                lambda e: e.collective_compute(
                    "AllGather", ALU.bypass, replica_groups=RG,
                    ins=[pr_in[w].opt()], outs=[pr_out[w].opt()]),
                reads=["prin"], writes=["prout"], collective=True)
        if dbg_stage == 14:
            dump(items_bf16=[(xsp, 0, 16448, 0), (gelu_g, 0, 16384, 16448)],
                 items_f32=[(prow, 0, 1024, 0)])
            return
        tr.flush()
        if dbg_stage == 2:
            return

        tr.barrier(["R0", "R1", "R2", "R3", "R4", "R5"])
        def gmu(e):
            for col in range(16):
                cci, src = col // 4, col % 4
                tr.idma(e, mu_ch[:, cci * T + src * TQ:cci * T + (src + 1) * TQ],
                        mu_out[w], IOff(idx_mu[:, col:col + 1], 0))
            return None
        tr.step("gpsimd", gmu, reads=["muout", "consts", "R3"],
                writes=[f"R3:mu{cci}" for cci in range(4)], fence=True)
        def gpr(e):
            for src in range(4):
                tr.dma(e, Bsb[0:16, src * TQ:(src + 1) * TQ],
                       pr_out[w][src * 33:src * 33 + 16, :])
                tr.dma(e, Csb[0:16, src * TQ:(src + 1) * TQ],
                       pr_out[w][src * 33 + 16:src * 33 + 32, :])
            return None
        tr.step("gpsimd", gpr, reads=["prout", "R1"], writes=["R1:Bsb", "R1:Csb"],
                fence=True)
        def bcst(e):
            tr.dma(e, Bc_d[:, :], Bsb[:, :])
            tr.dma(e, Cc_d[:, :], Csb[:, :])
            return None
        tr.step("sync", bcst, reads=["R1:Bsb", "R1:Csb"], writes=["BCd"])
        for tq in range(4):
            tr.step("sync",
                    lambda e, tq=tq: tr.dma(
                        e, f32tmp[:, 1024:2048],
                        bass.AP(pr_out_h[w], (tq * 33 + 32) * TQ,
                                [[1, 1], [0, 128], [1, 1024]])),
                    reads=["prout"], writes=["R7:q1"])
            def sig_cc(e, tq=tq):
                for cci in range(4):
                    r = e.activation(t1b[:, cci * 1024:(cci + 1) * 1024],
                                     f32tmp[:, 1024:2048], AF.Sigmoid,
                                     scale=negwdt_cs[:, cci:cci + 1],
                                     bias=negbdt_cs[:, cci:cci + 1])
                return r
            tr.step("scalar", sig_cc, reads=["R7:q1", "vconsts", "R0"],
                    writes=["R0:t1"])
            def ln_cc(e, tq=tq):
                for cci in range(4):
                    r = e.activation(wbig[:, cci * 4096 + tq * 1024:
                                          cci * 4096 + (tq + 1) * 1024],
                                     t1b[:, cci * 1024:(cci + 1) * 1024], AF.Ln)
                return r
            tr.step("scalar", ln_cc, reads=["R0:t1"],
                    writes=[f"R2:mdt{cci}_{tq}" for cci in range(4)])

        tr.barrier(["R0"])
        YW = [(0, 0, 512), (1, 512, 1024), (2, 1024, 2048), (4, 2048, 3072),
              (6, 3072, 4096)]
        for cci in range(4):
            def sgroup(e_dummy=None):
                pass
            for s in range(S + 1):
                p = s % 2
                if s < S:
                    def repdma(e, s=s, p=p):
                        tr.dma(e, repsF[:, p * T:(p + 1) * T],
                               bass.AP(Bc_dh, s * T, [[1, 1], [0, 128], [1, T]]))
                        tr.dma(e, repsF[:, (2 + p) * T:(3 + p) * T],
                               bass.AP(Cc_dh, s * T, [[1, 1], [0, 128], [1, T]]))
                        return None
                    tr.step("sync", repdma, reads=["BCd"],
                            writes=[f"R4:rb{p}", f"R4:rc{p}"])
                    tr.step("scalar",
                            lambda e, cci=cci, s=s, p=p: e.activation(
                                abuf[:, p * T:(p + 1) * T],
                                wbig[:, cci * 4096:(cci + 1) * 4096],
                                AF.Exp, scale=negA_cs[:, cci * S + s:cci * S + s + 1]),
                            reads=[f"R2:mdt{cci}_{tq}" for tq in range(4)] + ["negA"],
                            writes=[f"R5:a{p}"])
                    tr.step("gpsimd",
                            lambda e, cci=cci, p=p: e.tensor_tensor(
                                bbuf[:, p * T:(p + 1) * T],
                                mu_ch[:, cci * T:(cci + 1) * T],
                                repsF[:, p * T:(p + 1) * T], ALU.mult),
                            reads=[f"R3:mu{cci}", f"R4:rb{p}"], writes=[f"R0:b{p}"])
                    tr.step("vector",
                            lambda e, p=p: e.tensor_tensor_scan(
                                hbuf[:, p * T:(p + 1) * T], abuf[:, p * T:(p + 1) * T],
                                bbuf[:, p * T:(p + 1) * T], 0.0, ALU.mult, ALU.add),
                            reads=[f"R5:a{p}", f"R0:b{p}"], writes=[f"R0:h{p}"])
                if s >= 1:
                    sp = (s - 1) % 2
                    tr.step("vector",
                            lambda e, sp=sp: e.tensor_tensor(
                                bbuf[:, sp * T:(sp + 1) * T], hbuf[:, sp * T:(sp + 1) * T],
                                repsF[:, (2 + sp) * T:(3 + sp) * T], ALU.mult),
                            reads=[f"R0:h{sp}", f"R4:rc{sp}"], writes=[f"R0:b{sp}"])
                    def ymm(e, s1=s - 1, sp=sp):
                        for j in range(8):
                            r = e.matmul(BANK8[j], negeye[:, :],
                                         bbuf[:, sp * T + j * 512:sp * T + (j + 1) * 512],
                                         start=(s1 == 0), stop=(s1 == S - 1))
                        return r
                    tr.step("tensor", ymm, reads=[f"R0:b{sp}", "vconsts"], writes=BKEY)
            yk = ["R1:Bsb", "R1:Csb"][cci % 2]
            ysl = (cci % 2) * T
            def yev(e, ysl=ysl):
                e.activation(ystg[:, ysl:ysl + 512], pbA[:, :], AF.Identity)
                e.activation(ystg[:, ysl + 512:ysl + 1024], pbB[:, :], AF.Identity)
                e.activation(ystg[:, ysl + 1024:ysl + 2048], aps1[:, :], AF.Identity)
                e.activation(ystg[:, ysl + 2048:ysl + 3072], aps2[:, :], AF.Identity)
                return e.activation(ystg[:, ysl + 3072:ysl + 4096], yps[:, :], AF.Identity)
            tr.step("scalar", yev, reads=BKEY, writes=[yk])
            def yst_(e, cci=cci, ysl=ysl):
                for tq in range(4):
                    tr.dma(e, bass.AP(y_in_h[w], (cci * 512 + tq) * TQ,
                                      [[4 * TQ, 128], [1, TQ]]),
                           ystg[:, ysl + tq * 1024:ysl + (tq + 1) * 1024])
                return None
            tr.step("sync", yst_, reads=[yk], writes=[f"yin:{cci}"])
        tr.step("gpsimd",
                lambda e: e.collective_compute(
                    "AllGather", ALU.bypass, replica_groups=RG,
                    ins=[y_in[w].opt()], outs=[y_out[w].opt()]),
                reads=[f"yin:{cci}" for cci in range(4)],
                writes=["yout"], collective=True)
        if dbg_stage == 15:
            def ldy(e):
                for t_ in range(16):
                    tr.dma(e, repsF[:, t_ * 1024:(t_ + 1) * 1024],
                           y_in[w][t_ * 128:(t_ + 1) * 128, :])
                return None
            tr.step("sync", ldy, reads=["yout", "R4"], writes=["R4"])
            dump(items_bf16=[(wbig, 0, 16384, 0), (reps, 0, 16384, 16448)],
                 items_f32=[(mu_ch, 0, 4096, 0)] if False else [])
            return
        tr.flush()
        if dbg_stage == 3:
            return

        tr.barrier(["R0", "R1", "R2", "R4", "R5", "R7"])
        def ldx(e):
            for k in range(DMC):
                tr.dma(e, x_fm[:, k * 1024:(k + 1) * 1024], xfm_sp[k, :, :])
            return None
        tr.step("sync", ldx, reads=["xfmsp"], writes=[f"R0:x{k}" for k in range(DMC)])
        for t_ in range(DIC):
            ys = (t_ % 2) * 1024
            xs_ = 2048 + (t_ % 2) * 1024
            zs = 4096 + (t_ % 2) * 1024
            tr.step("gpsimd",
                    lambda e, t_=t_, ys=ys: tr.idma(
                        e, cgate[:, ys:ys + 1024], y_out[w],
                        IOff(idx_y[:, t_:t_ + 1], 0)),
                    reads=["yout", "consts", "R5"], writes=[f"R5:cy{t_ % 2}"],
                    fence=True)
            def ldg(e, t_=t_, xs_=xs_, zs=zs):
                tr.dma(e, cgate[:, xs_:xs_ + 1024], xc_sp[t_, :, :])
                tr.dma(e, cgate[:, zs:zs + 1024], sz_sp[t_, :, :])
                return None
            tr.step("sync", ldg, reads=[f"xcsp:{t_}", f"szsp:{t_}"],
                    writes=[f"R5:cx{t_ % 2}", f"R5:cz{t_ % 2}"])
            def vg(e, t_=t_, ys=ys, xs_=xs_, zs=zs):
                tp = 6144 + (t_ % 2) * 1024
                e.scalar_tensor_tensor(cgate[:, tp:tp + 1024],
                                       cgate[:, xs_:xs_ + 1024],
                                       lan["DLn"][:, t_:t_ + 1],
                                       cgate[:, ys:ys + 1024], ALU.mult, ALU.add)
                e.drain()
                return e.tensor_tensor(gelu_g[:, t_ * 1024:(t_ + 1) * 1024],
                                       cgate[:, tp:tp + 1024],
                                       cgate[:, zs:zs + 1024], ALU.mult)
            tr.step("vector", vg,
                    reads=[f"R5:cy{t_ % 2}", f"R5:cx{t_ % 2}", f"R5:cz{t_ % 2}",
                           "consts"],
                    writes=[f"R4:g{2*t_}", f"R4:g{2*t_+1}"])
        if dbg_stage == 20:
            def ldsp(e):
                tr.dma(e, cgate[:, 0:1024], xc_sp[14, :, :])
                tr.dma(e, cgate[:, 1024:2048], sz_sp[14, :, :])
                return None
            tr.step("sync", ldsp, reads=["xcsp:14", "szsp:14", "R5"], writes=["R5:cy0", "R5:cy1"])
            dump(items_bf16=[(gelu_g, 0, 16384, 0), (cgate, 0, 8192, 16448)],
                 items_f32=[(x_fm, 0, 4096, 0)])
            return
        def ldwo(e):
            for t_ in range(DIC):
                tr.dma(e, wbig[:, t_ * 1024:(t_ + 1) * 1024],
                       din["woutT"][t_ * 128:(t_ + 1) * 128, :])
            return None
        tr.step("gpsimd", ldwo, writes=["R2"], fence=True)
        for tc in range(2):
            for d in range(DMC):
                bank = [pbA, pbB][d % 2]
                bkey = ["P:a", "P:b"][d % 2]
                def mmo(e, tc=tc, d=d, bank=bank):
                    for t_ in range(DIC):
                        r = e.matmul(
                            bank[:, :],
                            wbig[:, t_ * 1024 + d * 128:t_ * 1024 + (d + 1) * 128],
                            gelu_g[:, t_ * 1024 + tc * 512:t_ * 1024 + tc * 512 + 512],
                            start=(t_ == 0), stop=(t_ == DIC - 1))
                    return r
                tr.step("tensor", mmo,
                        reads=["R2"] + [f"R4:g{gi}" for gi in range(32)],
                        writes=[bkey])
                tr.step("vector",
                        lambda e, tc=tc, d=d, bank=bank: e.tensor_tensor(
                            x_fm[:, d * 1024 + tc * 512:d * 1024 + tc * 512 + 512],
                            x_fm[:, d * 1024 + tc * 512:d * 1024 + tc * 512 + 512],
                            bank[:, :], ALU.add),
                        reads=[bkey], writes=[f"R0:x{d}"])
        if dbg_stage == 16:
            dump(items_f32=[(x_fm, 0, 8192, 0)], items_bf16=[(gelu_g, 0, 16384, 0)])
            return
        if dbg_stage == 4:
            return
        emit_LN(1e-5)
        if dbg_stage == 21:
            dump(items_f32=[(x_fm, 0, 8192, 0)], items_bf16=[(xn, 0, 8192, 0)])
            return
        emit_ffn(reload_w1=True)
        if dbg_stage == 22:
            dump(items_f32=[(x_fm, 0, 8192, 0)])
            return
        for tt in range(8):
            sl = (tt % 2) * 1024
            for k in range(DMC):
                bank = [aps1, aps2][k % 2]
                bkey = [("P:1l", "P:1h"), ("P:2l", "P:2h")][k % 2]
                tr.step("tensor",
                        lambda e, tt=tt, k=k, bank=bank: e.transpose(
                            bank[:, 0:128],
                            x_fm[:, k * 1024 + tt * 128:k * 1024 + (tt + 1) * 128],
                            eyeF[:, :]),
                        reads=[f"R0:x{k}", "consts"], writes=list(bkey))
                tr.step("scalar",
                        lambda e, k=k, sl=sl, bank=bank: e.activation(
                            f32tmp[:, sl + k * 128:sl + (k + 1) * 128],
                            bank[:, 0:128], AF.Identity),
                        reads=list(bkey), writes=[f"R7:o{tt % 2}_{k}"])
            tr.step("sync",
                    lambda e, tt=tt, sl=sl: tr.dma(
                        e, outs_w[w][tt * 128:(tt + 1) * 128, :], f32tmp[:, sl:sl + 1024]),
                    reads=[f"R7:o{tt % 2}_{k}" for k in range(DMC)],
                    writes=[f"out{w}:{tt}"])

    with tr.block():
        tr.barrier(["R1", "R2", "R3", "R4", "R5"])
        emit_wave(0)
        tr.flush()
        if dbg_stage == 0:
            tr.barrier(["R0", "R1", "R2", "R3", "R4", "R5", "R7"])
            emit_wave(1)
            tr.flush()

        def fin(e):
            e.wait_ge(tr.dsem["sync"].h, tr.dsem["sync"].n)
            e.wait_ge(tr.dsem["scalar"].h, tr.dsem["scalar"].n)
            e.wait_ge(tr.dsem["gpsimd"].h, tr.dsem["gpsimd"].n)
            return None
        tr.step("sync", fin)
        tr.flush()

    for sd in (tr.csem, tr.dsem):
        for e_, sem in sd.items():
            assert sem.n < 60000, f"sem overflow {e_} {sem.n}"
    return nc


def run_dbg(inputs, stage):
    per_core = host_prep(inputs)
    nc = build(stage)
    res = run_bass_kernel_spmd(nc, per_core, core_ids=list(range(8)))
    return res


def kernel(**inputs):
    per_core = host_prep(inputs)
    nc = build(0)
    res = run_bass_kernel_spmd(nc, per_core, core_ids=list(range(8)))
    out = np.empty((4, T, DM), np.float32)
    for c in range(8):
        q = c % 4
        for w, nm in enumerate(("outA", "outB")):
            out[2 * w + c // 4, q * TQ:(q + 1) * TQ, :] = res.results[c][nm]
    return out


if __name__ == "__main__":
    build(0)
    print("BUILD OK")


# version 04 (final)
# speedup vs baseline: 1.1195x; 1.1195x over previous
import numpy as np
import ml_dtypes
import concourse.bass as bass
import concourse.mybir as mybir
import bass_rust
from concourse.bass_utils import run_bass_kernel_spmd

F32 = mybir.dt.float32
BF16 = mybir.dt.bfloat16
FP8 = mybir.dt.float8e4
U32 = mybir.dt.uint32
AF = mybir.ActivationFunctionType
ALU = mybir.AluOpType
IOff = bass_rust.IndirectOffsetOnAxis

DM, DMC = 1024, 8
H, HC = 4096, 32
DI, DIC = 2048, 16
S = 16
TQ, T = 1024, 4096
NW = 2
XSW = 1028

ENGINES = ("sync", "scalar", "vector", "tensor", "gpsimd")


class Sem:
    def __init__(self, nc, name):
        self.h = nc.alloc_semaphore(name)
        self.n = 0
        self._name = name

    def b(self, k):
        self.n += k
        return self.h


class _ProbeInst:
    def then_inc(self, *a, **k):
        return self


class _Probe:
    def __getattr__(self, name):
        return lambda *a, **k: _ProbeInst()


class Tracker:

    def __init__(self, nc):
        self.nc = nc
        self.csem = {e: Sem(nc, "C_" + e) for e in ENGINES}
        self.dsem = {e: Sem(nc, "D_" + e) for e in ENGINES}
        self.cc = Sem(nc, "CCs")
        self.clock = {e: {} for e in ENGINES}
        self.res = {}
        self.nsteps = 0
        self._blk = None
        self.pending = {e: [] for e in ENGINES}

    def _evt(self, eng):
        s = self.csem[eng]
        return (s._name, s.h, s.n, dict(self.clock[eng]), eng)

    @staticmethod
    def _root(k):
        return k.split(":", 1)[0] if ":" in k else None

    def _dep_evts(self, eng, reads, writes, keep_same_engine=False):
        evts = []
        def rd(k):
            r = self.res.get(k)
            if r and r["w"]:
                evts.append(r["w"])
        def wr(k):
            r = self.res.get(k)
            if r:
                if r["w"]:
                    evts.append(r["w"])
                evts.extend(r["r"])
        for k in reads:
            rd(k)
            ro = self._root(k)
            if ro:
                rd(ro)
        for k in writes:
            wr(k)
            ro = self._root(k)
            if ro:
                r = self.res.get(ro)
                if r:
                    if r["w"]:
                        evts.append(r["w"])
                    evts.extend(r["r"])
            else:
                pref = k + ":"
                for k2 in list(self.res):
                    if k2.startswith(pref):
                        wr(k2)
        out = []
        for ev in evts:
            (sn, sh, cnt, snap, seng) = ev
            if cnt <= 0:
                continue
            if seng == eng and not keep_same_engine:
                continue
            if seng == eng and sn == self.dsem[eng]._name:
                continue
            if self.clock[eng].get(sn, 0) >= cnt:
                continue
            out.append(ev)
        return out

    def _merge_waits(self, eng, evts):
        bysem = {}
        for ev in evts:
            sn = ev[0]
            if sn not in bysem or ev[2] > bysem[sn][2]:
                bysem[sn] = ev
        waits = []
        for (sn, sh, cnt, snap, seng) in bysem.values():
            waits.append((sh, cnt))
            c = self.clock[eng]
            c[sn] = max(c.get(sn, 0), cnt)
            for k, v in snap.items():
                c[k] = max(c.get(k, 0), v)
        return waits

    def block(self):
        self._blk = self.nc.Block()
        return self._blk

    def step(self, eng, emit, reads=(), writes=(), collective=False, fence=False,
             qdma=False):
        reads = tuple(reads); writes = tuple(writes)
        deps = self._dep_evts(eng, reads, writes, keep_same_engine=qdma)
        waits = self._merge_waits(eng, deps)
        self._cur_eng = eng
        self._cur_dmas = 0
        self._probe = True
        try:
            emit(_Probe())
        finally:
            self._probe = False
        ndma = self._cur_dmas
        assert not (eng == "gpsimd" and ndma and not fence), "gpsimd dma needs fence"
        if ndma:
            self.dsem[eng].n += 16 * ndma
        if collective:
            self.cc.n += 1
            evt = (self.cc._name, self.cc.h, self.cc.n, dict(self.clock[eng]), "-cc-")
            inc = ("cc", self.cc.h, self.cc.n)
        elif ndma and eng in ("sync", "scalar") and not fence:
            s = self.dsem[eng]
            evt = (s._name, s.h, s.n, dict(self.clock[eng]), eng + "-q")
            inc = None
        elif ndma and fence:
            s = self.dsem[eng]
            self.clock[eng][s._name] = s.n
            self.csem[eng].n += 1
            evt = self._evt(eng)
            inc = ("fence", s.h, s.n, self.csem[eng].h, self.csem[eng].n)
        else:
            self.csem[eng].n += 1
            evt = self._evt(eng)
            inc = ("eng", self.csem[eng].h, self.csem[eng].n)
        self.pending[eng].append((emit, waits, inc, ndma))
        for k in writes:
            self.res[k] = {"w": evt, "r": []}
        for k in reads:
            self.res.setdefault(k, {"w": None, "r": []})["r"].append(evt)
        self.nsteps += 1

    def flush(self):
        tr = self
        for eng in ENGINES:
            steps = self.pending[eng]
            if not steps:
                continue
            self.pending[eng] = []
            def body(e, steps=steps, eng=eng):
                for (emit, waits, inc, ndma) in steps:
                    for (sh, cnt) in waits:
                        e.wait_ge(sh, cnt)
                    tr._cur_eng = eng
                    tr._cur_dmas = 0
                    last = emit(e)
                    assert tr._cur_dmas == ndma, \
                        f"{eng} step declared ndma={ndma} emitted {tr._cur_dmas}"
                    if inc is None:
                        continue
                    if inc[0] == "cc":
                        last.then_inc(inc[1], 1)
                    elif inc[0] == "fence":
                        e.wait_ge(inc[1], inc[2])
                        e.nop().then_inc(inc[3], 1)
                    else:
                        if last is None:
                            last = e.nop()
                        last.then_inc(inc[1], 1)
            getattr(self._blk, eng)(body)

    def dma(self, e, out, in_, **kw):
        self._cur_dmas += 1
        if getattr(self, "_probe", False):
            return _ProbeInst()
        s = self.dsem[self._cur_eng]
        return e.dma_start(out=out, in_=in_, **kw).then_inc(s.h, 16)

    def idma(self, e, out, in_, in_offset):
        self._cur_dmas += 1
        if getattr(self, "_probe", False):
            return _ProbeInst()
        s = self.dsem[self._cur_eng]
        return e.indirect_dma_start(out, None, in_, in_offset).then_inc(s.h, 16)

    def barrier(self, regions):
        for rg in regions:
            keys = [k for k in self.res if k == rg or k.startswith(rg + ":")]
            self.step("vector", lambda e: None, reads=keys, writes=[rg])


def host_prep(inp):
    asc = np.ascontiguousarray
    w1 = np.asarray(inp["L0_ffn_w1"]); w2 = np.asarray(inp["L0_ffn_w2"])
    win = np.asarray(inp["L0_attn_W_in"]); wout = np.asarray(inp["L0_attn_W_out"])
    wx = np.asarray(inp["L0_attn_W_x"]); x = np.asarray(inp["L0_attn_x"])
    common = {
        "w1T": asc(w1.T), "w1n": asc(w1), "w2T": asc(w2.T), "w2n": asc(w2),
        "winT": asc(win.T), "woutT": asc(wout.T), "wxT": asc(wx.T),
        "b1L": asc(np.asarray(inp["L0_ffn_b1"]).reshape(HC, 128).T),
        "b2L": asc(np.asarray(inp["L0_ffn_b2"]).reshape(DMC, 128).T),
        "convbL": asc(np.asarray(inp["L0_attn_conv_b"]).reshape(DIC, 128).T),
        "DLn": asc(np.asarray(inp["L0_attn_D"]).reshape(DIC, 128).T),
        "gL": asc(np.asarray(inp["L0_attn_g"]).reshape(DMC, 128).T),
        "wdtL": asc(np.asarray(inp["L0_attn_W_dt"])[:, 0].reshape(DIC, 128).T),
        "bdtL": asc(np.asarray(inp["L0_attn_b_dt"]).reshape(DIC, 128).T),
        "convwL": asc(np.asarray(inp["L0_attn_conv_w"]).reshape(DIC, 128, 4)
                      .transpose(1, 0, 2).reshape(128, DIC * 4)),
        "eyeB": np.eye(128, dtype=ml_dtypes.bfloat16),
        "eyeF": np.eye(128, dtype=np.float32),
    }
    alog = np.asarray(inp["L0_attn_A_log"])
    wdt = np.asarray(inp["L0_attn_W_dt"])[:, 0]
    bdt = np.asarray(inp["L0_attn_b_dt"])
    p_ = np.arange(128)
    per_core = []
    for c in range(8):
        q = c % 4
        d = dict(common)
        d["xA"] = asc(x[0 + c // 4, q * TQ:(q + 1) * TQ, :])
        d["xB"] = asc(x[2 + c // 4, q * TQ:(q + 1) * TQ, :])
        acs = np.zeros((128, 4 * S), np.float32)
        wcs = np.zeros((128, 4), np.float32)
        bcs = np.zeros((128, 4), np.float32)
        for cci in range(4):
            gt = q * 4 + cci
            acs[:, cci * S:(cci + 1) * S] = alog[gt * 128:(gt + 1) * 128, :]
            wcs[:, cci] = wdt[gt * 128:(gt + 1) * 128]
            bcs[:, cci] = bdt[gt * 128:(gt + 1) * 128]
        d["alog_cs"] = acs; d["wdt_cs"] = wcs; d["bdt_cs"] = bcs
        d["mask3"] = np.full((128, 1), 0.0 if q == 0 else 1.0, np.float32)
        im = np.zeros((128, 16), np.uint32)
        iy = np.zeros((128, 16), np.uint32)
        it = np.zeros((128, 16), np.uint32)
        for col in range(16):
            cci, src = col // 4, col % 4
            im[:, col] = src * DI + q * 512 + cci * 128 + p_
        for t in range(16):
            iy[:, t] = (t // 4) * DI + ((t % 4) * 128 + p_) * 4 + q
            it[:, t] = ((q + 3) % 4) * DI + t * 128 + p_
        d["idx_mu"] = im; d["idx_y"] = iy; d["idx_tail"] = it
        per_core.append(d)
    return per_core


def build(dbg_stage=0):
    nc = bass.Bass("TRN2", target_bir_lowering=False)
    din = {}
    def DIN(name, shape, dt=F32):
        din[name] = nc.dram_tensor(name, shape, dt, kind="ExternalInput").ap()
    DIN("xA", [TQ, DM]); DIN("xB", [TQ, DM])
    DIN("w1T", [DM, H]); DIN("w1n", [H, DM])
    DIN("w2T", [H, DM]); DIN("w2n", [DM, H])
    DIN("winT", [DM, H]); DIN("woutT", [DI, DM]); DIN("wxT", [DI, 33])
    for nm, w_ in [("b1L", HC), ("b2L", DMC), ("convbL", DIC), ("DLn", DIC),
                   ("gL", DMC), ("wdtL", DIC), ("bdtL", DIC), ("convwL", 4 * DIC),
                   ("alog_cs", 4 * S), ("wdt_cs", 4), ("bdt_cs", 4), ("mask3", 1)]:
        DIN(nm, [128, w_])
    DIN("eyeB", [128, 128], BF16); DIN("eyeF", [128, 128], F32)
    DIN("idx_mu", [128, 16], U32); DIN("idx_y", [128, 16], U32)
    DIN("idx_tail", [128, 16], U32)

    outs_w = [nc.dram_tensor("outA", [TQ, DM], F32, kind="ExternalOutput").ap(),
              nc.dram_tensor("outB", [TQ, DM], F32, kind="ExternalOutput").ap()]
    if dbg_stage:
        dbgF = nc.dram_tensor("dbgF", [128, 16448], F32, kind="ExternalOutput").ap()
        dbgB = nc.dram_tensor("dbgB", [128, 33000], BF16, kind="ExternalOutput").ap()

    w1s_d = nc.dram_tensor("w1s_d", [DMC, 128, H], FP8).ap()
    w2s_dh = nc.dram_tensor("w2s_d", [HC, 128, DM], FP8)
    w2s_d = w2s_dh.ap()
    xc_sp = nc.dram_tensor("xc_sp", [DIC, 128, TQ], BF16).ap()
    sz_sp = nc.dram_tensor("sz_sp", [DIC, 128, TQ], BF16).ap()
    xfm_sp = nc.dram_tensor("xfm_sp", [DMC, 128, TQ], F32).ap()
    strow_dh = nc.dram_tensor("strow_d", [4, 1024], F32)
    strow_d = strow_dh.ap()
    Bc_dh = nc.dram_tensor("Bc_d", [16, T], BF16)
    Cc_dh = nc.dram_tensor("Cc_d", [16, T], BF16)
    Bc_d = Bc_dh.ap(); Cc_d = Cc_dh.ap()
    tail_in = [nc.dram_tensor(f"tail_in{w}", [DI, 4], BF16).ap() for w in range(NW)]
    tail_out = [nc.dram_tensor(f"tail_out{w}", [4 * DI, 4], BF16).ap() for w in range(NW)]
    mu_in = [nc.dram_tensor(f"mu_in{w}", [DI, TQ], BF16).ap() for w in range(NW)]
    mu_out = [nc.dram_tensor(f"mu_out{w}", [4 * DI, TQ], BF16).ap() for w in range(NW)]
    pr_in_h = [nc.dram_tensor(f"pr_in{w}", [33, TQ], F32) for w in range(NW)]
    pr_in = [h.ap() for h in pr_in_h]
    pr_out_h = [nc.dram_tensor(f"pr_out{w}", [132, TQ], F32) for w in range(NW)]
    pr_out = [h.ap() for h in pr_out_h]
    y_in_h = [nc.dram_tensor(f"y_in{w}", [2048, TQ], BF16) for w in range(NW)]
    y_in = [h.ap() for h in y_in_h]
    y_out = [nc.dram_tensor(f"y_out{w}", [8192, TQ], BF16).ap() for w in range(NW)]

    A_ = nc.alloc_sbuf_tensor
    eyeB = A_("eyeB_s", [128, 128], BF16); eyeF = A_("eyeF_s", [128, 128], F32)
    negeye = A_("negeye_s", [128, 128], BF16)
    lan = {}
    for nm, w_ in [("b1L", HC), ("b2L", DMC), ("convbL", DIC), ("DLn", DIC),
                   ("gL", DMC), ("wdtL", DIC), ("bdtL", DIC), ("convwL", 4 * DIC),
                   ("alog_cs", 4 * S), ("wdt_cs", 4), ("bdt_cs", 4), ("mask3", 1)]:
        lan[nm] = A_(nm + "_s", [128, w_], F32)
    alpha1L = A_("alpha1L", [128, HC], F32); alpha2L = A_("alpha2L", [128, DMC], F32)
    b2hL = A_("b2hL", [128, DMC], F32)
    negwdtL = A_("negwdtL", [128, DIC], F32); negbdtL = A_("negbdtL", [128, DIC], F32)
    negwdt_cs = A_("negwdt_cs", [128, 4], F32); negbdt_cs = A_("negbdt_cs", [128, 4], F32)
    negA_cs = A_("negA_cs", [128, 4 * S], F32)
    onesL = A_("onesL", [128, 1], F32)
    epsP = A_("epsP", [128, 2], F32)
    idx_mu = A_("idx_mu_s", [128, 16], U32); idx_y = A_("idx_y_s", [128, 16], U32)
    idx_tail = A_("idx_tail_s", [128, 16], U32)
    state = A_("state_s", [128, 4 * S], F32)
    wxTs = A_("wxTs", [128, DIC * 33], BF16)
    junkrow = A_("junkrow", [128, 8], F32)

    KB = 1024
    ARENA = 193 * KB
    base = nc.sbuf_base
    arena_slab = nc.alloc_sbuf_tensor("arena_slab", [128, ARENA], mybir.dt.uint8)
    def place(name, shape, dt, off_b):
        assert off_b < ARENA
        return nc.alloc_sbuf_tensor_at(name, shape, dt, offset=base + off_b)
    x_fm   = place("x_fm",   [128, 8 * TQ], F32, 0)
    t1b    = place("t1b",    [128, 4096], F32, 0)
    bbuf   = place("bbuf",   [128, 2 * T], BF16, 0)
    hbuf   = place("hbuf",   [128, 2 * T], BF16, 16 * KB)
    xn     = place("xn",     [128, 8 * TQ], BF16, 32 * KB)
    xnf    = place("xnf",    [128, 4 * TQ], F32, 113 * KB)
    Bsb    = place("Bsb",    [16, T], BF16, 32 * KB)
    Csb    = place("Csb",    [16, T], BF16, 40 * KB)
    wbig   = place("wbig",   [128, 16384], BF16, 48 * KB)
    wbigF  = place("wbigF",  [128, 8192], F32, 48 * KB)
    xsp    = place("xsp",    [128, DIC * XSW], BF16, 80 * KB)
    mu_ch  = place("mu_ch",  [128, 4 * T], BF16, 80 * KB)
    gelu_g = place("gelu_g", [128, HC * 512], BF16, 113 * KB)
    repsF  = place("repsF",  [128, 4 * T], BF16, 113 * KB)
    w1s8   = place("w1s8",   [128, DMC * H], FP8, 145 * KB)
    diagc  = place("diagc",  [128, 64 * 128], BF16, 145 * KB)
    cgate  = place("cgate",  [128, 8 * TQ], BF16, 145 * KB)
    abuf   = place("abuf",   [128, 2 * T], F32, 145 * KB)
    ystg   = place("ystg",   [128, 2 * T], BF16, 32 * KB)
    f32tmp = place("f32tmp", [128, 4096], F32, 177 * KB)
    w2s8   = place("w2s8",   [128, 4 * 1024], FP8, 189 * KB)
    prow   = place("prow",   [33, TQ], F32, 189 * KB)
    st6    = place("st6",    [1, 3072], F32, 189 * KB)
    g8 = gelu_g.bitcast(FP8)

    aps1 = nc.alloc_psum_tensor("aps1", [128, 1024], F32)
    aps2 = nc.alloc_psum_tensor("aps2", [128, 1024], F32)
    yps = nc.alloc_psum_tensor("yps", [128, 1024], F32)
    pbA = nc.alloc_psum_tensor("pbA", [128, 512], F32)
    pbB = nc.alloc_psum_tensor("pbB", [128, 512], F32)
    BANK8 = [pbA[:, :], pbB[:, :], aps1[:, 0:512], aps1[:, 512:1024],
             aps2[:, 0:512], aps2[:, 512:1024], yps[:, 0:512], yps[:, 512:1024]]
    BKEY = ["P:a", "P:b", "P:1l", "P:1h", "P:2l", "P:2h", "P:yl", "P:yh"]

    tr = Tracker(nc)
    RG = [[0, 1, 2, 3], [4, 5, 6, 7]]

    def dump(items_f32=(), items_bf16=(), extra_reads=()):
        def d(e):
            for (t_, c0, c1, dc) in items_f32:
                tr.dma(e, dbgF[0:t_.shape[0], dc:dc + (c1 - c0)], t_[0:t_.shape[0], c0:c1])
            for (t_, c0, c1, dc) in items_bf16:
                tr.dma(e, dbgB[0:t_.shape[0], dc:dc + (c1 - c0)], t_[0:t_.shape[0], c0:c1])
            return None
        tr.step("sync", d, reads=list(extra_reads) +
                [k for k in tr.res if k.startswith(("R0", "R1", "R2", "R3", "R4",
                                                    "R5", "R7"))])

    with tr.block():
        def ld_consts(e):
            for nm in lan:
                tr.dma(e, lan[nm][:, :], din[nm])
            for nm, t_ in [("eyeB", eyeB), ("eyeF", eyeF), ("idx_mu", idx_mu),
                           ("idx_y", idx_y), ("idx_tail", idx_tail)]:
                tr.dma(e, t_[:, :], din[nm])
            return None
        tr.step("sync", ld_consts, writes=["consts"])

        def ld_wxT(e):
            for t_ in range(DIC):
                tr.dma(e, wxTs[:, t_ * 33:(t_ + 1) * 33],
                       din["wxT"][t_ * 128:(t_ + 1) * 128, :])
            return None
        tr.step("gpsimd", ld_wxT, writes=["wxTs"], fence=True)

        def v_consts(e):
            e.memset(onesL[:, :], 1.0)
            e.memset(epsP[:, 0:1], 1e-5)
            e.memset(epsP[:, 1:2], 1e-6)
            e.tensor_scalar(negwdtL[:, :], lan["wdtL"][:, :], -1.0, None, ALU.mult)
            e.tensor_scalar(negbdtL[:, :], lan["bdtL"][:, :], -1.0, None, ALU.mult)
            e.tensor_scalar(negwdt_cs[:, :], lan["wdt_cs"][:, :], -1.0, None, ALU.mult)
            e.tensor_scalar(negbdt_cs[:, :], lan["bdt_cs"][:, :], -1.0, None, ALU.mult)
            e.tensor_scalar(negeye[:, :], eyeB[:, :], -1.0, None, ALU.mult)
            return e.tensor_scalar(b2hL[:, :], lan["b2L"][:, :], 0.5, None, ALU.mult)
        tr.step("vector", v_consts, reads=["consts"], writes=["vconsts"])
        tr.step("scalar",
                lambda e: e.activation(negA_cs[:, :], lan["alog_cs"][:, :], AF.Exp),
                reads=["consts"], writes=["negA"])

        for half in range(2):
            ks = list(range(half * 4, half * 4 + 4))
            def ld_w1(e, ks=ks):
                for k in ks:
                    tr.dma(e, wbig[:, (k % 4) * 4096:(k % 4) * 4096 + 4096],
                           din["w1T"][k * 128:(k + 1) * 128, :])
                return None
            tr.step("gpsimd", ld_w1, writes=["R2"], fence=True)
            def sg_w1(e, ks=ks):
                for k in ks:
                    r = e.activation(w1s8[:, k * 4096:(k + 1) * 4096],
                                     wbig[:, (k % 4) * 4096:(k % 4) * 4096 + 4096],
                                     AF.Sign)
                return r
            tr.step("scalar", sg_w1, reads=["R2"], writes=["R5"])
        def st_w1(e):
            for k in range(DMC):
                tr.dma(e, w1s_d[k, :, :], w1s8[:, k * 4096:(k + 1) * 4096])
            return None
        tr.step("sync", st_w1, reads=["R5"], writes=["w1s_d"])

        for hh in range(HC):
            sl = (hh % 16) * 1024
            tr.step("gpsimd",
                    lambda e, hh=hh, sl=sl: tr.dma(
                        e, wbig[:, sl:sl + 1024], din["w2T"][hh * 128:(hh + 1) * 128, :]),
                    writes=[f"R2:w2_{hh % 16}"], fence=True)
            tr.step("scalar",
                    lambda e, hh=hh, sl=sl: e.activation(
                        g8[:, hh * 1024:(hh + 1) * 1024], wbig[:, sl:sl + 1024], AF.Sign),
                    reads=[f"R2:w2_{hh % 16}"], writes=[f"R4:w2s_{hh}"])
            tr.step("sync",
                    lambda e, hh=hh: tr.dma(e, w2s_d[hh, :, :],
                                            g8[:, hh * 1024:(hh + 1) * 1024]),
                    reads=[f"R4:w2s_{hh}"], writes=[f"w2s_d:{hh // 4}_{hh % 4}"])

        for t_ in range(HC):
            sl = (t_ % 2) * 1024
            tr.step("gpsimd",
                    lambda e, t_=t_, sl=sl: tr.dma(
                        e, xsp[:, sl:sl + 1024], din["w1n"][t_ * 128:(t_ + 1) * 128, :]),
                    writes=[f"R3:a{t_ % 2}"], fence=True)
            tr.step("scalar",
                    lambda e, t_=t_, sl=sl: e.activation(
                        xn[:, 0:1024], xsp[:, sl:sl + 1024], AF.Abs,
                        accum_out=alpha1L[:, t_:t_ + 1]),
                    reads=[f"R3:a{t_ % 2}"], writes=["R1:abs", f"al1:{t_}"])
        for t_ in range(DMC):
            sl = (t_ % 2) * 4096
            tr.step("gpsimd",
                    lambda e, t_=t_, sl=sl: tr.dma(
                        e, xsp[:, sl:sl + 4096], din["w2n"][t_ * 128:(t_ + 1) * 128, :]),
                    writes=[f"R3:b{t_ % 2}"], fence=True)
            tr.step("scalar",
                    lambda e, t_=t_, sl=sl: e.activation(
                        xn[:, 0:4096], xsp[:, sl:sl + 4096], AF.Abs,
                        accum_out=alpha2L[:, t_:t_ + 1]),
                    reads=[f"R3:b{t_ % 2}"], writes=["R1:abs", f"al2:{t_}"])
        def v_alpha(e):
            e.tensor_scalar(alpha1L[:, :], alpha1L[:, :], 1.0 / DM, None, ALU.mult)
            return e.tensor_scalar(alpha2L[:, :], alpha2L[:, :], 0.5 / H, None, ALU.mult)
        tr.step("vector", v_alpha,
                reads=[f"al1:{t_}" for t_ in range(HC)] +
                      [f"al2:{t_}" for t_ in range(DMC)],
                writes=["alphas"])
        tr.flush()

    def emit_LN(eps, rms_after=False):
        for k in range(DMC):
            tr.step("scalar",
                    lambda e, k=k: e.activation(f32tmp[:, 0:1024],
                                                x_fm[:, k * 1024:(k + 1) * 1024],
                                                AF.Square),
                    reads=[f"R0:x{k}"], writes=["R7:q0"])
            def mms(e, k=k):
                st = (k == 0); sp = (k == DMC - 1)
                e.matmul(pbA[0:1, :], onesL[:, :], x_fm[:, k * 1024:k * 1024 + 512],
                         start=st, stop=sp)
                e.matmul(pbB[0:1, :], onesL[:, :],
                         x_fm[:, k * 1024 + 512:(k + 1) * 1024], start=st, stop=sp)
                e.matmul(aps1[0:1, 0:512], onesL[:, :], f32tmp[:, 0:512],
                         start=st, stop=sp)
                return e.matmul(aps1[0:1, 512:1024], onesL[:, :], f32tmp[:, 512:1024],
                                start=st, stop=sp)
            tr.step("tensor", mms, reads=["R7:q0", f"R0:x{k}", "vconsts"],
                    writes=["P:a", "P:b", "P:1l", "P:1h"])
        def v_stats(e):
            e.tensor_scalar(st6[0:1, 0:512], pbA[0:1, :], 1.0 / DM, None, ALU.mult)
            e.tensor_scalar(st6[0:1, 512:1024], pbB[0:1, :], 1.0 / DM, None, ALU.mult)
            e.tensor_scalar(st6[0:1, 1024:1536], aps1[0:1, 0:512], 1.0 / DM, None,
                            ALU.mult)
            e.tensor_scalar(st6[0:1, 1536:2048], aps1[0:1, 512:1024], 1.0 / DM, None,
                            ALU.mult)
            e.tensor_tensor(st6[0:1, 2048:3072], st6[0:1, 0:1024], st6[0:1, 0:1024],
                            ALU.mult)
            return e.tensor_tensor(st6[0:1, 1024:2048], st6[0:1, 1024:2048],
                                   st6[0:1, 2048:3072], ALU.subtract)
        tr.step("vector", v_stats, reads=["P:a", "P:b", "P:1l", "P:1h"],
                writes=["R7:q3_0", "R7:q3_1", "R7:q3_2"])
        tr.step("scalar",
                lambda e: e.activation(st6[0:1, 1024:2048], st6[0:1, 1024:2048],
                                       AF.Sqrt, bias=epsP[0:1, 0:1]),
                reads=["R7:q3_1", "vconsts"], writes=["R7:q3_1"])
        def v_rinv(e):
            e.reciprocal(st6[0:1, 2048:3072], st6[0:1, 1024:2048])
            e.tensor_copy(st6[0:1, 1024:2048], st6[0:1, 2048:3072])
            return e.tensor_tensor(st6[0:1, 2048:3072], st6[0:1, 0:1024],
                                   st6[0:1, 2048:3072], ALU.mult)
        tr.step("vector", v_rinv, reads=["R7:q3_1", "R7:q3_0"],
                writes=["R7:q3_1", "R7:q3_2"])
        def bc0(e):
            tr.dma(e, strow_d[0:1, :], st6[0:1, 1024:2048])
            tr.dma(e, strow_d[1:2, :], st6[0:1, 2048:3072])
            return None
        tr.step("sync", bc0, reads=["R7:q3_1", "R7:q3_2"], writes=["strow"])
        def bc(e):
            tr.dma(e, f32tmp[:, 1024:2048],
                   bass.AP(strow_dh, 0, [[1, 1], [0, 128], [1, 1024]]))
            tr.dma(e, f32tmp[:, 2048:3072],
                   bass.AP(strow_dh, 1024, [[1, 1], [0, 128], [1, 1024]]))
            return None
        tr.step("sync", bc, reads=["strow"], writes=["R7:q1", "R7:q2"])

        if not rms_after:
            for k in range(DMC):
                def v_app(e, k=k):
                    sl = (k % 4) * 1024
                    e.tensor_tensor(xnf[:, sl:sl + 1024],
                                    x_fm[:, k * 1024:(k + 1) * 1024],
                                    f32tmp[:, 1024:2048], ALU.mult)
                    return e.tensor_tensor(xn[:, k * 1024:(k + 1) * 1024],
                                           xnf[:, sl:sl + 1024],
                                           f32tmp[:, 2048:3072], ALU.subtract)
                j = k % 4
                tr.step("vector", v_app, reads=["R7:q1", "R7:q2", f"R0:x{k}"],
                        writes=[f"R1:n{k}", f"R4:g{2*j}", f"R4:g{2*j+1}"])
        else:
            for k in range(DMC):
                def v_app2(e, k=k):
                    sl = (k % 4) * 1024
                    e.tensor_tensor(xnf[:, sl:sl + 1024],
                                    x_fm[:, k * 1024:(k + 1) * 1024],
                                    f32tmp[:, 1024:2048], ALU.mult)
                    e.tensor_tensor(xnf[:, sl:sl + 1024], xnf[:, sl:sl + 1024],
                                    f32tmp[:, 2048:3072], ALU.subtract)
                    e.tensor_copy(xn[:, k * 1024:(k + 1) * 1024], xnf[:, sl:sl + 1024])
                    return e.tensor_tensor(x_fm[:, k * 1024:(k + 1) * 1024],
                                           x_fm[:, k * 1024:(k + 1) * 1024],
                                           xnf[:, sl:sl + 1024], ALU.add)
                j = k % 4
                tr.step("vector", v_app2, reads=["R7:q1", "R7:q2"],
                        writes=[f"R0:x{k}", f"R1:n{k}", f"R4:g{2*j}", f"R4:g{2*j+1}"])
                tr.step("scalar",
                        lambda e, k=k: e.activation(
                            f32tmp[:, 0:1024],
                            xnf[:, (k % 4) * 1024:(k % 4) * 1024 + 1024], AF.Square),
                        reads=[f"R4:g{2*(k%4)}", f"R4:g{2*(k%4)+1}"], writes=["R7:q0"])
                def mms2(e, k=k):
                    st = (k == 0); sp = (k == DMC - 1)
                    e.matmul(pbA[0:1, :], onesL[:, :], f32tmp[:, 0:512], start=st, stop=sp)
                    return e.matmul(pbB[0:1, :], onesL[:, :], f32tmp[:, 512:1024],
                                    start=st, stop=sp)
                tr.step("tensor", mms2, reads=["R7:q0", "vconsts"], writes=["P:a", "P:b"])
            def v_ms(e):
                e.tensor_scalar(st6[0:1, 0:512], pbA[0:1, :], 1.0 / DM, None, ALU.mult)
                return e.tensor_scalar(st6[0:1, 512:1024], pbB[0:1, :], 1.0 / DM, None,
                                       ALU.mult)
            tr.step("vector", v_ms, reads=["P:a", "P:b"], writes=["R7:q3_0"])
            tr.step("scalar",
                    lambda e: e.activation(st6[0:1, 0:1024], st6[0:1, 0:1024], AF.Sqrt,
                                           bias=epsP[0:1, 1:2]),
                    reads=["R7:q3_0", "vconsts"], writes=["R7:q3_0"])
            tr.step("vector",
                    lambda e: e.reciprocal(st6[0:1, 1024:2048], st6[0:1, 0:1024]),
                    reads=["R7:q3_0"], writes=["R7:q3_1"])
            tr.step("sync",
                    lambda e: tr.dma(e, strow_d[2:3, :], st6[0:1, 1024:2048]),
                    reads=["R7:q3_1"], writes=["strow"])
            tr.step("sync",
                    lambda e: tr.dma(e, f32tmp[:, 1024:2048],
                                     bass.AP(strow_dh, 2048, [[1, 1], [0, 128], [1, 1024]])),
                    reads=["strow"], writes=["R7:q1"])
            for k in range(DMC):
                tr.step("vector",
                        lambda e, k=k: e.scalar_tensor_tensor(
                            xn[:, k * 1024:(k + 1) * 1024],
                            xn[:, k * 1024:(k + 1) * 1024],
                            lan["gL"][:, k:k + 1], f32tmp[:, 1024:2048],
                            ALU.mult, ALU.mult),
                        reads=["R7:q1", f"R1:n{k}", "consts"], writes=[f"R1:n{k}"])

    def emit_ffn(reload_w1):
        if reload_w1:
            def ld1(e):
                for k in range(DMC):
                    tr.dma(e, w1s8[:, k * 4096:(k + 1) * 4096], w1s_d[k, :, :])
                return None
            tr.step("sync", ld1, reads=["w1s_d"], writes=["R5"])
        for tc in range(2):
            for ht in range(HC):
                bank = [pbA, pbB][ht % 2]
                bkey = ["P:a", "P:b"][ht % 2]
                def mm1(e, ht=ht, tc=tc, bank=bank):
                    for k in range(DMC):
                        r = e.matmul(bank[:, :],
                                     w1s8[:, k * 4096 + ht * 128:k * 4096 + (ht + 1) * 128],
                                     xn[:, k * 1024 + tc * 512:k * 1024 + tc * 512 + 512],
                                     start=(k == 0), stop=(k == DMC - 1))
                    return r
                tr.step("tensor", mm1,
                        reads=["R5"] + [f"R1:n{k}" for k in range(DMC)], writes=[bkey])
                tr.step("scalar",
                        lambda e, ht=ht, bank=bank: e.activation(
                            gelu_g[:, ht * 512:(ht + 1) * 512], bank[:, :], AF.Gelu,
                            bias=lan["b1L"][:, ht:ht + 1], scale=alpha1L[:, ht:ht + 1]),
                        reads=[bkey, "alphas", "consts"], writes=[f"R4:g{ht}"])
            for hg in range(HC // 4):
                def ldw2(e, hg=hg):
                    tr.dma(e, w2s8[:, :],
                           bass.AP(w2s_dh, hg * 4 * 128 * DM,
                                   [[DM, 128], [128 * DM, 4], [1, DM]]))
                    return None
                tr.step("sync", ldw2,
                        reads=[f"w2s_d:{hg}_{i}" for i in range(4)],
                        writes=["R7:q3_0", "R7:q3_1", "R7:q3_2", "R7:q3_3"])
                def mm2(e, hg=hg):
                    for q4 in range(4):
                        hh = hg * 4 + q4
                        for d in range(DMC):
                            r = e.matmul(BANK8[d],
                                         w2s8[:, q4 * 1024 + d * 128:q4 * 1024 + (d + 1) * 128],
                                         gelu_g[:, hh * 512:(hh + 1) * 512],
                                         start=(hh == 0), stop=(hh == HC - 1))
                    return r
                tr.step("tensor", mm2,
                        reads=["R7:q3_0", "R7:q3_1", "R7:q3_2", "R7:q3_3"] +
                              [f"R4:g{hg * 4 + i}" for i in range(4)],
                        writes=BKEY)
            for d in range(DMC):
                def vev(e, d=d, tc=tc):
                    sl = 2048 + (d % 2) * 512
                    e.tensor_scalar(f32tmp[:, sl:sl + 512], BANK8[d],
                                    alpha2L[:, d:d + 1], b2hL[:, d:d + 1],
                                    ALU.mult, ALU.add)
                    dst = x_fm[:, d * 1024 + tc * 512:d * 1024 + tc * 512 + 512]
                    return e.tensor_tensor(dst, dst, f32tmp[:, sl:sl + 512], ALU.add)
                tr.step("vector", vev, reads=[BKEY[d], "alphas", f"R0:x{d}"],
                        writes=[f"R0:x{d}", "R7:q2"])

    def emit_wave(w):
        xin = din["xA"] if w == 0 else din["xB"]

        for tt in range(8):
            sl = (tt % 2) * 1024
            tr.step("sync",
                    lambda e, tt=tt, sl=sl: tr.dma(
                        e, f32tmp[:, sl:sl + 1024], xin[tt * 128:(tt + 1) * 128, :]),
                    writes=[f"R7:q{tt % 2}"])
            for k in range(DMC):
                bank = [aps1, aps2][k % 2]
                bkey = [("P:1l", "P:1h"), ("P:2l", "P:2h")][k % 2]
                tr.step("tensor",
                        lambda e, tt=tt, sl=sl, k=k, bank=bank: e.transpose(
                            bank[:, 0:128], f32tmp[:, sl + k * 128:sl + (k + 1) * 128],
                            eyeF[:, :]),
                        reads=[f"R7:q{tt % 2}", "consts"], writes=list(bkey))
                tr.step("scalar",
                        lambda e, tt=tt, k=k, bank=bank: e.activation(
                            x_fm[:, k * 1024 + tt * 128:k * 1024 + (tt + 1) * 128],
                            bank[:, 0:128], AF.Identity),
                        reads=list(bkey), writes=[f"R0:x{k}"])

        emit_LN(1e-5)
        if dbg_stage == 11:
            dump(items_f32=[(x_fm, 0, 8192, 0)], items_bf16=[(xn, 0, 8192, 0)])
            return
        emit_ffn(reload_w1=(w > 0))
        if dbg_stage == 12:
            dump(items_f32=[(x_fm, 0, 8192, 0)])
            return
        emit_LN(1e-5, rms_after=True)
        if dbg_stage == 13:
            dump(items_f32=[(x_fm, 0, 8192, 0)], items_bf16=[(xn, 0, 8192, 0)])
            return

        def v_diag(e):
            for t_ in range(DIC):
                for j in range(4):
                    r = e.tensor_scalar(
                        diagc[:, (t_ * 4 + j) * 128:(t_ * 4 + j + 1) * 128],
                        eyeB[:, :], lan["convwL"][:, t_ * 4 + j:t_ * 4 + j + 1],
                        None, ALU.mult)
            return r
        tr.step("vector", v_diag, reads=["consts"], writes=["R5:dg"])

        for half in range(2):
            def ldw(e, half=half):
                for k in range(DMC):
                    tr.dma(e, wbig[:, k * 2048:(k + 1) * 2048],
                           din["winT"][k * 128:(k + 1) * 128,
                                       half * 2048:(half + 1) * 2048])
                return None
            tr.step("gpsimd", ldw, writes=["R2"], fence=True)
            for oc in range(DIC):
                bank = [pbA, pbB][oc % 2]
                bkey = ["P:a", "P:b"][oc % 2]
                for tc in range(2):
                    def mmw(e, oc=oc, tc=tc, bank=bank):
                        for k in range(DMC):
                            r = e.matmul(
                                bank[:, :],
                                wbig[:, k * 2048 + oc * 128:k * 2048 + (oc + 1) * 128],
                                xn[:, k * 1024 + tc * 512:k * 1024 + tc * 512 + 512],
                                start=(k == 0), stop=(k == DMC - 1))
                        return r
                    tr.step("tensor", mmw,
                            reads=["R2"] + [f"R1:n{k}" for k in range(DMC)],
                            writes=[bkey])
                    if half == 0:
                        tr.step("scalar",
                                lambda e, oc=oc, tc=tc, bank=bank: e.activation(
                                    xsp[:, oc * XSW + 4 + tc * 512:
                                        oc * XSW + 4 + tc * 512 + 512],
                                    bank[:, :], AF.Identity),
                                reads=[bkey], writes=[f"R3:xs{oc}_{tc}"])
                    else:
                        tr.step("scalar",
                                lambda e, oc=oc, tc=tc, bank=bank: e.activation(
                                    gelu_g[:, oc * 1024 + tc * 512:
                                           oc * 1024 + tc * 512 + 512],
                                    bank[:, :], AF.Silu),
                                reads=[bkey], writes=[f"R4:g{2*oc+tc}"])
                if half == 1:
                    tr.step("scalar",
                            lambda e, oc=oc: tr.dma(
                                e, sz_sp[oc, :, :], gelu_g[:, oc * 1024:(oc + 1) * 1024]),
                            reads=[f"R4:g{2*oc}", f"R4:g{2*oc+1}"], writes=[f"szsp:{oc}"],
                            qdma=True)

        def tails(e):
            for t_ in range(DIC):
                tr.dma(e, tail_in[w][t_ * 128:(t_ + 1) * 128, :],
                       xsp[:, t_ * XSW + 4 + 1020:t_ * XSW + 4 + 1024])
            return None
        tr.step("sync", tails, reads=[f"R3:xs{t_}_1" for t_ in range(DIC)],
                writes=["tin"])
        tr.step("gpsimd",
                lambda e: e.collective_compute(
                    "AllGather", ALU.bypass, replica_groups=RG,
                    ins=[tail_in[w].opt()], outs=[tail_out[w].opt()]),
                reads=["tin"], writes=["tout"], collective=True)
        def halo(e):
            for t_ in range(DIC):
                tr.idma(e, xsp[:, t_ * XSW:t_ * XSW + 4], tail_out[w],
                        IOff(idx_tail[:, t_:t_ + 1], 0))
            return None
        tr.step("gpsimd", halo, reads=["tout", "consts"],
                writes=[f"R3:h{t_}" for t_ in range(DIC)], fence=True)
        def vmask(e):
            for t_ in range(DIC):
                r = e.tensor_scalar(xsp[:, t_ * XSW:t_ * XSW + 4],
                                    xsp[:, t_ * XSW:t_ * XSW + 4],
                                    lan["mask3"][:, 0:1], None, ALU.mult)
            return r
        tr.step("vector", vmask,
                reads=[f"R3:h{t_}" for t_ in range(DIC)] + ["consts"], writes=["R3:hm"])

        for t_ in range(DIC):
            bank = [pbA, pbB][t_ % 2]
            bkey = ["P:a", "P:b"][t_ % 2]
            for tc in range(2):
                def mmc(e, t_=t_, tc=tc, bank=bank):
                    for j in range(4):
                        r = e.matmul(
                            bank[:, :],
                            diagc[:, (t_ * 4 + j) * 128:(t_ * 4 + j + 1) * 128],
                            xsp[:, t_ * XSW + 1 + j + tc * 512:
                                t_ * XSW + 1 + j + tc * 512 + 512],
                            start=(j == 0), stop=(j == 3))
                    return r
                tr.step("tensor", mmc,
                        reads=["R5:dg", "R3:hm", f"R3:xs{t_}_0", f"R3:xs{t_}_1"],
                        writes=[bkey])
                tr.step("scalar",
                        lambda e, t_=t_, tc=tc, bank=bank: e.activation(
                            gelu_g[:, t_ * 1024 + tc * 512:t_ * 1024 + tc * 512 + 512],
                            bank[:, :], AF.Silu, bias=lan["convbL"][:, t_:t_ + 1]),
                        reads=[bkey, "consts", f"szsp:{t_}"],
                        writes=[f"R4:g{2*t_+tc}"])
            tr.step("scalar",
                    lambda e, t_=t_: tr.dma(e, xc_sp[t_, :, :],
                                            gelu_g[:, t_ * 1024:(t_ + 1) * 1024]),
                    reads=[f"R4:g{2*t_}", f"R4:g{2*t_+1}"], writes=[f"xcsp:{t_}"],
                    qdma=True)

        for tc in range(2):
            bank = [pbA, pbB][tc]
            bkey = ["P:a", "P:b"][tc]
            def mmx(e, tc=tc, bank=bank):
                for t_ in range(DIC):
                    r = e.matmul(bank[0:33, :], wxTs[:, t_ * 33:(t_ + 1) * 33],
                                 gelu_g[:, t_ * 1024 + tc * 512:t_ * 1024 + tc * 512 + 512],
                                 start=(t_ == 0), stop=(t_ == DIC - 1))
                return r
            tr.step("tensor", mmx,
                    reads=["wxTs"] + [f"R4:g{2*t_+tc}" for t_ in range(DIC)],
                    writes=[bkey])
            tr.step("scalar",
                    lambda e, tc=tc, bank=bank: e.activation(
                        prow[:, tc * 512:(tc + 1) * 512], bank[0:33, :], AF.Identity),
                    reads=[bkey], writes=[f"R7:q3_{tc}", f"prow{tc}"])
        tr.step("sync", lambda e: tr.dma(e, pr_in[w][:, :], prow[:, :]),
                reads=["prow0", "prow1"], writes=["prin"])

        tr.step("sync",
                lambda e, w=w: tr.dma(e, f32tmp[:, 0:1024],
                                 bass.AP(pr_in_h[w], 32 * TQ, [[1, 1], [0, 128], [1, 1024]])),
                reads=["prin"], writes=["R7:q0"])
        for g in range(4):
            ts = [g * 4 + i for i in range(4)]
            def sig4(e, ts=ts):
                for i in range(4):
                    r = e.activation(wbigF[:, i * 1024:(i + 1) * 1024], f32tmp[:, 0:1024],
                                     AF.Sigmoid, scale=negwdtL[:, ts[i]:ts[i] + 1],
                                     bias=negbdtL[:, ts[i]:ts[i] + 1])
                return r
            tr.step("scalar", sig4, reads=["R7:q0", "vconsts"],
                    writes=[f"R2:f{i}" for i in range(4)])
            for i, t_ in enumerate(ts):
                tr.step("scalar",
                        lambda e, i=i: e.activation(f32tmp[:, 1024:2048],
                                                    wbigF[:, i * 1024:(i + 1) * 1024],
                                                    AF.Ln),
                        reads=[f"R2:f{i}"], writes=["R7:q1"])
                tr.step("vector",
                        lambda e, t_=t_: e.tensor_tensor(
                            f32tmp[:, 2048:2560].bitcast(BF16), f32tmp[:, 1024:2048],
                            gelu_g[:, t_ * 1024:(t_ + 1) * 1024], ALU.mult),
                        reads=["R7:q1", f"R4:g{2*t_}", f"R4:g{2*t_+1}"],
                        writes=["R7:q2"])
                tr.step("sync",
                        lambda e, t_=t_: tr.dma(e, mu_in[w][t_ * 128:(t_ + 1) * 128, :],
                                                f32tmp[:, 2048:2560].bitcast(BF16)),
                        reads=["R7:q2"], writes=[f"muin:{t_}"])

        def spx(e):
            for k in range(DMC):
                tr.dma(e, xfm_sp[k, :, :], x_fm[:, k * 1024:(k + 1) * 1024])
            return None
        tr.step("scalar", spx, reads=[f"R0:x{k}" for k in range(DMC)], writes=["xfmsp"],
                qdma=True)

        tr.step("gpsimd",
                lambda e: e.collective_compute(
                    "AllGather", ALU.bypass, replica_groups=RG,
                    ins=[mu_in[w].opt()], outs=[mu_out[w].opt()]),
                reads=[f"muin:{t_}" for t_ in range(DIC)], writes=["muout"],
                collective=True)
        tr.step("gpsimd",
                lambda e: e.collective_compute(
                    "AllGather", ALU.bypass, replica_groups=RG,
                    ins=[pr_in[w].opt()], outs=[pr_out[w].opt()]),
                reads=["prin"], writes=["prout"], collective=True)
        if dbg_stage == 14:
            dump(items_bf16=[(xsp, 0, 16448, 0), (gelu_g, 0, 16384, 16448)],
                 items_f32=[(prow, 0, 1024, 0)])
            return
        tr.flush()
        if dbg_stage == 2:
            return

        tr.barrier(["R0", "R1", "R2", "R3", "R4", "R5"])
        def gmu(e):
            for col in range(16):
                cci, src = col // 4, col % 4
                tr.idma(e, mu_ch[:, cci * T + src * TQ:cci * T + (src + 1) * TQ],
                        mu_out[w], IOff(idx_mu[:, col:col + 1], 0))
            return None
        tr.step("gpsimd", gmu, reads=["muout", "consts", "R3"],
                writes=[f"R3:mu{cci}" for cci in range(4)], fence=True)
        def gpr(e):
            for src in range(4):
                tr.dma(e, Bsb[0:16, src * TQ:(src + 1) * TQ],
                       pr_out[w][src * 33:src * 33 + 16, :])
                tr.dma(e, Csb[0:16, src * TQ:(src + 1) * TQ],
                       pr_out[w][src * 33 + 16:src * 33 + 32, :])
            return None
        tr.step("gpsimd", gpr, reads=["prout", "R1"], writes=["R1:Bsb", "R1:Csb"],
                fence=True)
        def bcst(e):
            tr.dma(e, Bc_d[:, :], Bsb[:, :])
            tr.dma(e, Cc_d[:, :], Csb[:, :])
            return None
        tr.step("sync", bcst, reads=["R1:Bsb", "R1:Csb"], writes=["BCd"])
        for tq in range(4):
            tr.step("sync",
                    lambda e, tq=tq: tr.dma(
                        e, f32tmp[:, 1024:2048],
                        bass.AP(pr_out_h[w], (tq * 33 + 32) * TQ,
                                [[1, 1], [0, 128], [1, 1024]])),
                    reads=["prout"], writes=["R7:q1"])
            def sig_cc(e, tq=tq):
                for cci in range(4):
                    r = e.activation(t1b[:, cci * 1024:(cci + 1) * 1024],
                                     f32tmp[:, 1024:2048], AF.Sigmoid,
                                     scale=negwdt_cs[:, cci:cci + 1],
                                     bias=negbdt_cs[:, cci:cci + 1])
                return r
            tr.step("scalar", sig_cc, reads=["R7:q1", "vconsts", "R0"],
                    writes=["R0:t1"])
            def ln_cc(e, tq=tq):
                for cci in range(4):
                    r = e.activation(wbig[:, cci * 4096 + tq * 1024:
                                          cci * 4096 + (tq + 1) * 1024],
                                     t1b[:, cci * 1024:(cci + 1) * 1024], AF.Ln)
                return r
            tr.step("scalar", ln_cc, reads=["R0:t1"],
                    writes=[f"R2:mdt{cci}_{tq}" for cci in range(4)])

        tr.barrier(["R0"])
        YW = [(0, 0, 512), (1, 512, 1024), (2, 1024, 2048), (4, 2048, 3072),
              (6, 3072, 4096)]
        for cci in range(4):
            for s in range(S):
                p = s % 2
                def repdma(e, s=s, p=p):
                    tr.dma(e, repsF[:, p * T:(p + 1) * T],
                           bass.AP(Bc_dh, s * T, [[1, 1], [0, 128], [1, T]]))
                    tr.dma(e, repsF[:, (2 + p) * T:(3 + p) * T],
                           bass.AP(Cc_dh, s * T, [[1, 1], [0, 128], [1, T]]))
                    return None
                tr.step("sync", repdma, reads=["BCd"],
                        writes=[f"R4:rb{p}", f"R4:rc{p}"])
                tr.step("scalar",
                        lambda e, cci=cci, s=s, p=p: e.activation(
                            abuf[:, p * T:(p + 1) * T],
                            wbig[:, cci * 4096:(cci + 1) * 4096],
                            AF.Exp, scale=negA_cs[:, cci * S + s:cci * S + s + 1]),
                        reads=[f"R2:mdt{cci}_{tq}" for tq in range(4)] + ["negA"],
                        writes=[f"R5:a{p}"])
                tr.step("gpsimd",
                        lambda e, cci=cci, p=p: e.tensor_tensor(
                            bbuf[:, p * T:(p + 1) * T],
                            mu_ch[:, cci * T:(cci + 1) * T],
                            repsF[:, p * T:(p + 1) * T], ALU.mult),
                        reads=[f"R3:mu{cci}", f"R4:rb{p}"], writes=[f"R0:b{p}"])
                def vscan(e, p=p):
                    e.tensor_tensor_scan(hbuf[:, p * T:(p + 1) * T],
                                         abuf[:, p * T:(p + 1) * T],
                                         bbuf[:, p * T:(p + 1) * T],
                                         0.0, ALU.mult, ALU.add)
                    return e.tensor_copy(junkrow[:, 0:1], negA_cs[:, 0:1])
                tr.step("vector", vscan,
                        reads=[f"R5:a{p}", f"R0:b{p}", "negA"], writes=[f"R0:h{p}"])
                tr.step("vector",
                        lambda e, p=p: e.tensor_tensor(
                            bbuf[:, p * T:(p + 1) * T], hbuf[:, p * T:(p + 1) * T],
                            repsF[:, (2 + p) * T:(3 + p) * T], ALU.mult),
                        reads=[f"R0:h{p}", f"R4:rc{p}"], writes=[f"R0:b{p}"])
                def ymm(e, s=s, p=p):
                    for j in range(8):
                        r = e.matmul(BANK8[j], negeye[:, :],
                                     bbuf[:, p * T + j * 512:p * T + (j + 1) * 512],
                                     start=(s == 0), stop=(s == S - 1))
                    return r
                tr.step("tensor", ymm, reads=[f"R0:b{p}", "vconsts"], writes=BKEY)
            yk = ["R1:Bsb", "R1:Csb"][cci % 2]
            ysl = (cci % 2) * T
            def yev(e, ysl=ysl):
                e.activation(ystg[:, ysl:ysl + 512], pbA[:, :], AF.Identity)
                e.activation(ystg[:, ysl + 512:ysl + 1024], pbB[:, :], AF.Identity)
                e.activation(ystg[:, ysl + 1024:ysl + 2048], aps1[:, :], AF.Identity)
                e.activation(ystg[:, ysl + 2048:ysl + 3072], aps2[:, :], AF.Identity)
                return e.activation(ystg[:, ysl + 3072:ysl + 4096], yps[:, :], AF.Identity)
            tr.step("scalar", yev, reads=BKEY, writes=[yk])
            def yst_(e, cci=cci, ysl=ysl):
                for tq in range(4):
                    tr.dma(e, bass.AP(y_in_h[w], (cci * 512 + tq) * TQ,
                                      [[4 * TQ, 128], [1, TQ]]),
                           ystg[:, ysl + tq * 1024:ysl + (tq + 1) * 1024])
                return None
            tr.step("sync", yst_, reads=[yk], writes=[f"yin:{cci}"])
        tr.step("gpsimd",
                lambda e: e.collective_compute(
                    "AllGather", ALU.bypass, replica_groups=RG,
                    ins=[y_in[w].opt()], outs=[y_out[w].opt()]),
                reads=[f"yin:{cci}" for cci in range(4)],
                writes=["yout"], collective=True)
        if dbg_stage == 15:
            def ldy(e):
                for t_ in range(16):
                    tr.dma(e, repsF[:, t_ * 1024:(t_ + 1) * 1024],
                           y_in[w][t_ * 128:(t_ + 1) * 128, :])
                return None
            tr.step("sync", ldy, reads=["yout", "R4"], writes=["R4"])
            dump(items_bf16=[(wbig, 0, 16384, 0), (reps, 0, 16384, 16448)],
                 items_f32=[(mu_ch, 0, 4096, 0)] if False else [])
            return
        tr.flush()
        if dbg_stage == 3:
            return

        tr.barrier(["R0", "R1", "R2", "R4", "R5", "R7"])
        def ldx(e):
            for k in range(DMC):
                tr.dma(e, x_fm[:, k * 1024:(k + 1) * 1024], xfm_sp[k, :, :])
            return None
        tr.step("sync", ldx, reads=["xfmsp"], writes=[f"R0:x{k}" for k in range(DMC)])
        for t_ in range(DIC):
            ys = (t_ % 2) * 1024
            xs_ = 2048 + (t_ % 2) * 1024
            zs = 4096 + (t_ % 2) * 1024
            tr.step("gpsimd",
                    lambda e, t_=t_, ys=ys: tr.idma(
                        e, cgate[:, ys:ys + 1024], y_out[w],
                        IOff(idx_y[:, t_:t_ + 1], 0)),
                    reads=["yout", "consts", "R5"], writes=[f"R5:cy{t_ % 2}"],
                    fence=True)
            def ldg(e, t_=t_, xs_=xs_, zs=zs):
                tr.dma(e, cgate[:, xs_:xs_ + 1024], xc_sp[t_, :, :])
                tr.dma(e, cgate[:, zs:zs + 1024], sz_sp[t_, :, :])
                return None
            tr.step("sync", ldg, reads=[f"xcsp:{t_}", f"szsp:{t_}"],
                    writes=[f"R5:cx{t_ % 2}", f"R5:cz{t_ % 2}"])
            def vg(e, t_=t_, ys=ys, xs_=xs_, zs=zs):
                tp = 6144 + (t_ % 2) * 1024
                e.scalar_tensor_tensor(cgate[:, tp:tp + 1024],
                                       cgate[:, xs_:xs_ + 1024],
                                       lan["DLn"][:, t_:t_ + 1],
                                       cgate[:, ys:ys + 1024], ALU.mult, ALU.add)
                e.drain()
                return e.tensor_tensor(gelu_g[:, t_ * 1024:(t_ + 1) * 1024],
                                       cgate[:, tp:tp + 1024],
                                       cgate[:, zs:zs + 1024], ALU.mult)
            tr.step("vector", vg,
                    reads=[f"R5:cy{t_ % 2}", f"R5:cx{t_ % 2}", f"R5:cz{t_ % 2}",
                           "consts"],
                    writes=[f"R4:g{2*t_}", f"R4:g{2*t_+1}"])
        if dbg_stage == 20:
            def ldsp(e):
                tr.dma(e, cgate[:, 0:1024], xc_sp[14, :, :])
                tr.dma(e, cgate[:, 1024:2048], sz_sp[14, :, :])
                return None
            tr.step("sync", ldsp, reads=["xcsp:14", "szsp:14", "R5"], writes=["R5:cy0", "R5:cy1"])
            dump(items_bf16=[(gelu_g, 0, 16384, 0), (cgate, 0, 8192, 16448)],
                 items_f32=[(x_fm, 0, 4096, 0)])
            return
        def ldwo(e):
            for t_ in range(DIC):
                tr.dma(e, wbig[:, t_ * 1024:(t_ + 1) * 1024],
                       din["woutT"][t_ * 128:(t_ + 1) * 128, :])
            return None
        tr.step("gpsimd", ldwo, writes=["R2"], fence=True)
        for tc in range(2):
            for d in range(DMC):
                bank = [pbA, pbB][d % 2]
                bkey = ["P:a", "P:b"][d % 2]
                def mmo(e, tc=tc, d=d, bank=bank):
                    for t_ in range(DIC):
                        r = e.matmul(
                            bank[:, :],
                            wbig[:, t_ * 1024 + d * 128:t_ * 1024 + (d + 1) * 128],
                            gelu_g[:, t_ * 1024 + tc * 512:t_ * 1024 + tc * 512 + 512],
                            start=(t_ == 0), stop=(t_ == DIC - 1))
                    return r
                tr.step("tensor", mmo,
                        reads=["R2"] + [f"R4:g{gi}" for gi in range(32)],
                        writes=[bkey])
                tr.step("vector",
                        lambda e, tc=tc, d=d, bank=bank: e.tensor_tensor(
                            x_fm[:, d * 1024 + tc * 512:d * 1024 + tc * 512 + 512],
                            x_fm[:, d * 1024 + tc * 512:d * 1024 + tc * 512 + 512],
                            bank[:, :], ALU.add),
                        reads=[bkey], writes=[f"R0:x{d}"])
        if dbg_stage == 16:
            dump(items_f32=[(x_fm, 0, 8192, 0)], items_bf16=[(gelu_g, 0, 16384, 0)])
            return
        if dbg_stage == 4:
            return
        emit_LN(1e-5)
        if dbg_stage == 21:
            dump(items_f32=[(x_fm, 0, 8192, 0)], items_bf16=[(xn, 0, 8192, 0)])
            return
        emit_ffn(reload_w1=True)
        if dbg_stage == 22:
            dump(items_f32=[(x_fm, 0, 8192, 0)])
            return
        for tt in range(8):
            sl = (tt % 2) * 1024
            for k in range(DMC):
                bank = [aps1, aps2][k % 2]
                bkey = [("P:1l", "P:1h"), ("P:2l", "P:2h")][k % 2]
                tr.step("tensor",
                        lambda e, tt=tt, k=k, bank=bank: e.transpose(
                            bank[:, 0:128],
                            x_fm[:, k * 1024 + tt * 128:k * 1024 + (tt + 1) * 128],
                            eyeF[:, :]),
                        reads=[f"R0:x{k}", "consts"], writes=list(bkey))
                tr.step("scalar",
                        lambda e, k=k, sl=sl, bank=bank: e.activation(
                            f32tmp[:, sl + k * 128:sl + (k + 1) * 128],
                            bank[:, 0:128], AF.Identity),
                        reads=list(bkey), writes=[f"R7:o{tt % 2}_{k}"])
            tr.step("sync",
                    lambda e, tt=tt, sl=sl: tr.dma(
                        e, outs_w[w][tt * 128:(tt + 1) * 128, :], f32tmp[:, sl:sl + 1024]),
                    reads=[f"R7:o{tt % 2}_{k}" for k in range(DMC)],
                    writes=[f"out{w}:{tt}"])

    with tr.block():
        tr.barrier(["R1", "R2", "R3", "R4", "R5"])
        emit_wave(0)
        tr.flush()
        if dbg_stage == 0:
            tr.barrier(["R0", "R1", "R2", "R3", "R4", "R5", "R7"])
            emit_wave(1)
            tr.flush()

        def fin(e):
            e.wait_ge(tr.dsem["sync"].h, tr.dsem["sync"].n)
            e.wait_ge(tr.dsem["scalar"].h, tr.dsem["scalar"].n)
            e.wait_ge(tr.dsem["gpsimd"].h, tr.dsem["gpsimd"].n)
            return None
        tr.step("sync", fin)
        tr.flush()

    for sd in (tr.csem, tr.dsem):
        for e_, sem in sd.items():
            assert sem.n < 60000, f"sem overflow {e_} {sem.n}"
    return nc


def run_dbg(inputs, stage):
    per_core = host_prep(inputs)
    nc = build(stage)
    res = run_bass_kernel_spmd(nc, per_core, core_ids=list(range(8)))
    return res


def kernel(**inputs):
    per_core = host_prep(inputs)
    nc = build(0)
    res = run_bass_kernel_spmd(nc, per_core, core_ids=list(range(8)))
    out = np.empty((4, T, DM), np.float32)
    for c in range(8):
        q = c % 4
        for w, nm in enumerate(("outA", "outB")):
            out[2 * w + c // 4, q * TQ:(q + 1) * TQ, :] = res.results[c][nm]
    return out


if __name__ == "__main__":
    build(0)
    print("BUILD OK")
